# version 27
# baseline (speedup 1.0000x reference)
import numpy as np
import ml_dtypes
import concourse.bass as bass
import concourse.mybir as mybir
from concourse.bass_utils import run_bass_kernel_spmd

F32 = mybir.dt.float32
BF16 = mybir.dt.bfloat16
I16 = mybir.dt.int16
AF = mybir.ActivationFunctionType
OP = mybir.AluOpType
AX = mybir.AxisListType

S = 2048
D = 1024
NH = 16
DH = 64
NR = 1369
PFX = S - NR
DI = 2816
NCORES = 8
CMAG = 1.5 * 2 ** 23

ENGS = ("T", "V", "S", "G", "Y")
_PHASES = 99
_DEBUG = ()
_LSTRIPS = 4
_ROPE = True


class Sched:
    def __init__(self, sems):
        self.prog = {e: [] for e in ENGS}
        self.cnt = {n: 0 for n in sems}
        self.sems = sems

    def emit(self, eng, fn):
        self.prog[eng].append(fn)

    def wait(self, eng, name, val=None):
        v = self.cnt[name] if val is None else val
        if not v or v <= 0:
            return
        self.emit(eng, lambda e, s=self.sems[name], v=v: e.wait_ge(s, v))

    def op(self, eng, fn, inc=None, n=1):
        if inc is not None:
            self.cnt[inc] += n
            self.emit(eng, lambda e, f=fn, s=self.sems[inc], n=n: f(e).then_inc(s, n))
            return self.cnt[inc]
        self.emit(eng, lambda e, f=fn: f(e))
        return None


def build_nc():
    nc = bass.Bass("TRN2", target_bir_lowering=False, debug=False)

    x_h = nc.dram_tensor("x", [S, D], F32, kind="ExternalInput").ap()
    sinT_h = nc.dram_tensor("sinT", [DH, NR], F32, kind="ExternalInput").ap()
    cosT_h = nc.dram_tensor("cosT", [DH, NR], F32, kind="ExternalInput").ap()
    wqkv_h = nc.dram_tensor("wqkv", [D, 3 * D], F32, kind="ExternalInput").ap()
    bqkv_h = nc.dram_tensor("bqkv_pp", [128, 24], F32, kind="ExternalInput").ap()
    brows_h = nc.dram_tensor("brows", [1, 2048], F32, kind="ExternalInput").ap()
    wproj_h = nc.dram_tensor("wproj", [D, D], F32, kind="ExternalInput").ap()
    wg_h = nc.dram_tensor("wg", [DI, D], F32, kind="ExternalInput").ap()
    wu_h = nc.dram_tensor("wu", [DI, D], F32, kind="ExternalInput").ap()
    wd_h = nc.dram_tensor("wd", [D, DI], F32, kind="ExternalInput").ap()
    ones_h = nc.dram_tensor("ones", [128, 128], BF16, kind="ExternalInput").ap()
    ident_h = nc.dram_tensor("ident", [128, 128], BF16, kind="ExternalInput").ap()
    swap_h = nc.dram_tensor("swap", [128, 128], BF16, kind="ExternalInput").ap()
    smask_h = nc.dram_tensor("smask", [128, 1], I16, kind="ExternalInput").ap()
    out_h = nc.dram_tensor("out", [S, D], F32, kind="ExternalOutput").ap()
    x2_d = nc.dram_tensor("x2_d", [S, D], F32, kind="Internal").ap()
    ln2q_d = nc.dram_tensor("ln2q_d", [S, D], BF16, kind="Internal").ap()
    hq_d = nc.dram_tensor("hq_d", [S, DI], BF16, kind="Internal").ap()
    wgq_d = nc.dram_tensor("wgq_d", [DI, D], BF16, kind="Internal").ap()
    wuq_d = nc.dram_tensor("wuq_d", [DI, D], BF16, kind="Internal").ap()
    wdq_d = nc.dram_tensor("wdq_d", [D, DI], BF16, kind="Internal").ap()

    from contextlib import ExitStack
    stk = ExitStack()
    sb = lambda n, s, d: stk.enter_context(nc.sbuf_tensor(n, s, d))
    psa = lambda n, s, d: stk.enter_context(nc.psum_tensor(n, s, d))

    ARN_COLS = 72960
    ARN = sb("arena", [128, ARN_COLS], BF16)
    Q0 = 0
    K0 = 16384
    V0 = 32768
    SC0 = 49408
    WS0 = 52146
    LN0 = 60338
    XS0 = 68530
    FG0 = 0
    FU0 = 22528
    FL0 = 45056
    FH0 = 53248
    QS0 = FH0 + 8448


    rowbank = sb("rowbank", [128, 2048], BF16)
    onesb = sb("onesb", [128, 128], BF16)
    identb = sb("identb", [128, 128], BF16)
    swapb = sb("swapb", [128, 128], BF16)
    smask = sb("smaskb", [128, 1], I16)
    bqkv = sb("bqkvb", [128, 24], F32)
    miscb = sb("miscbf", [128, 8], BF16)
    xt = sb("xt", [128, 2 * D], F32)
    ot = sb("ot", [128, 2 * D], F32)
    dd = sb("ddt", [128, 1408], F32)
    stats = sb("stats", [128, 12], F32)
    mv = sb("mv", [128, 2], F32)
    rstd = sb("rstdt", [128, 2], F32)
    bm = sb("bmt", [128, 176], F32)
    rcp = sb("rcpt", [128, 176], F32)
    rs2 = sb("rs2t", [128, 176], F32)
    sc2 = sb("sc2t", [128, 176], F32)
    sgn = sb("sgnt", [128, 1408], I16)
    sgnb = sb("sgnbt", [128, 1408], BF16)
    misc = sb("miscf", [128, 64], F32)
    mrow = sb("mrowf", [1, 512], F32)
    r_sb = sb("r_sb", [65, 1024], BF16)
    recip = sb("recipt", [64, 512], F32)
    recipb = sb("recipbt", [64, 512], F32)
    betat = sb("betat", [64, 2], F32)
    betatA = sb("betatA", [64, 16], F32)
    onesf = sb("onesft", [128, 64], F32)
    otmp = sb("otmpt", [64, 1024], BF16)

    psSS = psa("psSS", [128, 2048], F32)
    psO = psa("psO", [128, 1024], F32)
    psX = psa("psX", [128, 512], F32)
    psT = psa("psT", [128, 1024], BF16)

    sems = {}
    for name in ("tS", "vS", "sS", "gS", "dY", "dG"):
        sems[name] = stk.enter_context(nc.semaphore(name))
    k = Sched(sems)
    T, V, SE, G, Y = ENGS

    def dY(fn):
        return k.op(Y, fn, inc="dY", n=16)

    def dG(fn):
        return k.op(G, fn, inc="dG", n=16)

    dY(lambda e: e.dma_start(out=onesb[:], in_=ones_h))
    dY(lambda e: e.dma_start(out=identb[:], in_=ident_h))
    dY(lambda e: e.dma_start(out=swapb[:], in_=swap_h))
    dY(lambda e: e.dma_start(out=smask[:], in_=smask_h))
    dY(lambda e: e.dma_start(out=bqkv[:], in_=bqkv_h))
    dG(lambda e: e.dma_start(out=rowbank[0:1, 0:2048], in_=brows_h))
    dG(lambda e: e.dma_start(out=ARN[0:64, SC0:SC0 + NR], in_=sinT_h))
    dG(lambda e: e.dma_start(out=ARN[64:128, SC0:SC0 + NR], in_=sinT_h))
    dG(lambda e: e.dma_start(out=ARN[0:64, SC0 + NR:SC0 + 2 * NR], in_=cosT_h))
    consts_dG = dG(lambda e: e.dma_start(out=ARN[64:128, SC0 + NR:SC0 + 2 * NR], in_=cosT_h))
    consts_dY = k.cnt["dY"]
    k.op(G, lambda e: e.memset(onesf[:], 1.0), inc="gS")
    onesf_gS = k.cnt["gS"]

    ARNF = ARN[:].bitcast(F32)
    QSC = [QS0 + i * 2816 for i in range(4)]
    qtrack = {"v": 0, "s": 0, "g": 0}

    def quant(src, dst, F, src_v=0, qsc=None):
        sc_ = QSC if qsc is None else qsc
        t_ = ARNF[:, sc_[0] // 2:sc_[0] // 2 + F]
        b1 = ARNF[:, sc_[1] // 2:sc_[1] // 2 + F]
        b2 = ARNF[:, sc_[2] // 2:sc_[2] // 2 + F]
        b3 = ARNF[:, sc_[3] // 2:sc_[3] // 2 + F]
        nb = F // 16
        bmv, rcpv, rs2v, sc2v = bm[:, 0:nb], rcp[:, 0:nb], rs2[:, 0:nb], sc2[:, 0:nb]
        sgv, ddv = sgn[:, 0:F], dd[:, 0:F]
        zb = sgnb[:, 0:F]
        k.wait(V, "gS", qtrack["g"])
        k.wait(V, "sS", qtrack["s"])
        k.op(V, lambda e: e.tensor_reduce(bmv, src.rearrange("p (nb b) -> p nb b", b=16),
                                          axis=AX.X, op=OP.max, apply_absolute_value=True),
             inc="vS")
        k.wait(V, "vS")
        k.op(V, lambda e: e.reciprocal(rcpv, bmv), inc="vS")
        k.wait(V, "vS")
        k.op(V, lambda e: e.tensor_scalar(rs2v, rcpv, 6.0, 6.0e12, OP.mult, OP.min))
        k.op(V, lambda e: e.tensor_scalar(sc2v, bmv, 1.0 / 6.0, 1e-12, OP.mult, OP.max),
             inc="vS")
        k.wait(V, "vS")
        k.op(V, lambda e: e.tensor_copy(zb, src))
        k.wait(SE, "gS", qtrack["g"])
        k.wait(SE, "vS", max(qtrack["v"], src_v))
        s_abs = k.op(SE, lambda e: e.activation(out=b1, in_=src, func=AF.Abs), inc="sS")
        k.wait(V, "sS", s_abs)
        rsb = rs2v.unsqueeze(-1).broadcast_to([128, nb, 16])
        vt = k.op(V, lambda e: e.tensor_tensor(t_.rearrange("p (nb b) -> p nb b", b=16),
                                               b1.rearrange("p (nb b) -> p nb b", b=16),
                                               rsb, op=OP.mult), inc="vS")
        k.wait(G, "vS", vt)
        k.op(G, lambda e: e.tensor_scalar(b1, t_, 2.0, 4.0, OP.mult, OP.add))
        k.op(G, lambda e: e.tensor_scalar(b2, t_, 8.0, 1.0, OP.add, OP.mult))
        gd1 = k.op(G, lambda e: e.tensor_scalar(b3, t_, 4.0, 1.0, OP.mult, OP.mult), inc="gS")
        k.wait(V, "gS", gd1)
        k.op(V, lambda e: e.tensor_tensor(b2, b2, b1, op=OP.min))
        vmin = k.op(V, lambda e: e.tensor_tensor(b3, b3, b2, op=OP.min), inc="vS")
        k.wait(SE, "vS", vmin)
        k.op(SE, lambda e: e.activation(out=ddv, in_=b3, func=AF.Copy, bias=CMAG, scale=0.5))
        vbf = ARN[:, sc_[2]:sc_[2] + F]
        g1b = ARN[:, sc_[2] + F:sc_[2] + 2 * F]
        h1b = ARN[:, sc_[1]:sc_[1] + F]
        h2b = ARN[:, sc_[1] + F:sc_[1] + 2 * F]
        s_v = k.op(SE, lambda e: e.activation(out=vbf, in_=ddv, func=AF.Copy,
                                              bias=-CMAG / 2, scale=0.5), inc="sS")
        k.wait(G, "sS", s_v)
        k.op(G, lambda e: e.tensor_scalar(h1b, vbf, 2.0, -2.0, OP.mult, OP.add))
        g_done = k.op(G, lambda e: e.tensor_scalar(h2b, vbf, 4.0, -8.0, OP.mult, OP.add),
                      inc="gS")
        k.wait(V, "gS", g_done)
        k.op(V, lambda e: e.tensor_tensor(g1b, vbf, h1b, op=OP.max))
        k.op(V, lambda e: e.tensor_tensor(h1b, g1b, h2b, op=OP.max))
        scb = sc2v.unsqueeze(-1).broadcast_to([128, nb, 16])
        k.op(V, lambda e: e.tensor_tensor(dst.rearrange("p (nb b) -> p nb b", b=16),
                                          h1b.rearrange("p (nb b) -> p nb b", b=16),
                                          scb, op=OP.mult))
        k.op(V, lambda e: e.tensor_scalar(sgv, zb.bitcast(I16), smask[:, 0:1], None,
                                          OP.bitwise_and, OP.bypass))
        v_done = k.op(V, lambda e: e.tensor_tensor(dst.bitcast(I16), dst.bitcast(I16), sgv,
                                                   op=OP.bitwise_or), inc="vS")
        qtrack.update(v=v_done, s=s_v, g=g_done)
        return v_done, s_abs, g_done

    wjobs = []
    for i in range(22):
        wjobs.append((wg_h[i * 128:(i + 1) * 128, :], wgq_d[i * 128:(i + 1) * 128, :], 1024))
    for i in range(22):
        wjobs.append((wu_h[i * 128:(i + 1) * 128, :], wuq_d[i * 128:(i + 1) * 128, :], 1024))
    for i in range(8):
        for c0, c1 in ((0, 1024), (1024, 2048), (2048, 2816)):
            wjobs.append((wd_h[i * 128:(i + 1) * 128, c0:c1],
                          wdq_d[i * 128:(i + 1) * 128, c0:c1], c1 - c0))
    WQS = [SC0, LN0, LN0 + 2048, LN0 + 4096]
    WRAW = [LN0 + 6144, LN0 + 8192]
    WQO = [LN0 + 10240, LN0 + 11264]
    wstate = {"i": 0, "ld": [0, 0], "q": [None, None], "st": [0, 0], "fence": 0}

    def emit_wjob():
        i = wstate["i"]
        if i >= len(wjobs):
            return
        if i == 0:
            fence = wstate["fence"]
            k.wait(G, "tS", fence)
            k.wait(V, "tS", fence)
            k.wait(SE, "tS", fence)
        wstate["i"] = i + 1
        srcap, dstap, F = wjobs[i]
        par = i % 2
        raw = ARNF[:, WRAW[par] // 2:WRAW[par] // 2 + F]
        qout = ARN[:, WQO[par]:WQO[par] + F]
        if wstate["q"][par] is not None:
            vd, sa, gd = wstate["q"][par]
            k.wait(G, "vS", vd)
            k.wait(G, "sS", sa)
        ld = dG(lambda e, r=raw, s=srcap: e.dma_start(out=r, in_=s))
        k.wait(V, "dG", ld)
        k.wait(SE, "dG", ld)
        if wstate["st"][par]:
            k.wait(V, "dY", wstate["st"][par])
        wstate["q"][par] = quant(raw, qout, F, qsc=WQS)
        k.wait(Y, "vS", wstate["q"][par][0])
        wstate["st"][par] = dY(lambda e, q=qout, dst=dstap: e.dma_start(out=dst, in_=q))

    if _PHASES >= 2:
        k.wait(T, "dY", consts_dY)
        k.wait(T, "dG", consts_dG)
        lastvln = [0, 0]
        tr_done = 0
        psT_evac = 0
        ln1_done = [0] * 4
        strip_mm = [0] * 4
        wbuf_free = [0, 0]
        QKV_PS = (psSS[:, 0:512], psSS[:, 512:1024], psSS[:, 1024:1536],
                  psSS[:, 1536:2048], psO[:, 0:512], psO[:, 512:1024])
        rot = 0
        ps_free = [0] * 6
        for st_ in range(_LSTRIPS):
            lnb_off = LN0 + (st_ % 2) * 4096
            for ti in range(4):
                g_ti = st_ * 4 + ti
                par = g_ti % 2
                t0 = st_ * 512 + ti * 128
                xv = xt[:, par * D:par * D + D]
                if g_ti >= 2:
                    k.wait(Y, "vS", lastvln[par])
                dx = dY(lambda e, x=xv, t0=t0: e.dma_start(out=x, in_=x_h[t0:t0 + 128, :]))
                k.wait(V, "dY", dx)
                k.op(V, lambda e, x=xv: e.bn_stats(stats[:, 0:6], x[:, 0:512]), inc="vS")
                k.wait(V, "vS")
                k.op(V, lambda e, x=xv: e.bn_stats(stats[:, 6:12], x[:, 512:1024]), inc="vS")
                k.wait(V, "vS")
                k.op(V, lambda e: e.bn_aggr(mv[:], stats[:]), inc="vS")
                k.wait(V, "vS")
                veps = k.op(V, lambda e: e.tensor_scalar(mv[:, 1:2], mv[:, 1:2], 1e-5, None,
                                                         OP.add, OP.bypass), inc="vS")
                k.wait(SE, "vS", veps)
                ssq = k.op(SE, lambda e: e.activation(out=rstd[:, 0:1], in_=mv[:, 1:2],
                                                      func=AF.Sqrt), inc="sS")
                k.wait(V, "sS", ssq)
                k.op(V, lambda e: e.reciprocal(rstd[:, 1:2], rstd[:, 0:1]), inc="vS")
                k.wait(V, "vS")
                lnt = ARN[:, XS0:XS0 + 1024]
                if g_ti >= 1:
                    k.wait(V, "tS", tr_done)
                vln = k.op(V, lambda e, x=xv, o=lnt: e.tensor_scalar(
                    o, x, mv[:, 0:1], rstd[:, 1:2], OP.subtract, OP.mult), inc="vS")
                lastvln[par] = vln
                k.wait(T, "vS", vln)
                if g_ti >= 1:
                    k.wait(T, "sS", psT_evac)
                for c in range(8):
                    fn = lambda e, c=c, o=lnt: e.transpose(
                        psT[:, c * 128:(c + 1) * 128], o[:, c * 128:(c + 1) * 128], identb[:])
                    if c == 7:
                        tr_done = k.op(T, fn, inc="tS")
                    else:
                        k.op(T, fn)
                k.wait(SE, "tS", tr_done)
                if st_ >= 2 and ti == 0:
                    k.wait(SE, "tS", strip_mm[st_ - 2])
                psT_evac = k.op(SE, lambda e, ti=ti, off=lnb_off: e.copy(
                    out=ARN[:, off:off + 4096].rearrange("p (c w) -> p c w", w=512)
                        [:, :, ti * 128:(ti + 1) * 128],
                    in_=psT[:].rearrange("p (c w) -> p c w", w=128)), inc="sS")
            ln1_done[st_] = psT_evac

            for grp in range(6):
                wpar = grp % 2
                woff = WS0 + wpar * 4096
                k.wait(G, "tS", wbuf_free[wpar])
                lds = None
                for c in range(8):
                    lds = dG(lambda e, c=c, g=grp, o=woff: e.dma_start(
                        out=ARN[:, o + c * 512:o + (c + 1) * 512],
                        in_=wqkv_h[c * 128:(c + 1) * 128, g * 512:(g + 1) * 512]))
                k.wait(T, "dG", lds)
                k.wait(T, "sS", ln1_done[st_])
                if grp < 4:
                    for jj in range(4):
                        j = grp * 4 + jj
                        pi = rot % 6
                        rot += 1
                        ps = QKV_PS[pi]
                        k.wait(T, "vS", ps_free[pi])
                        for c in range(8):
                            fn = lambda e, c=c, jj=jj, ps=ps, o=woff, lb=lnb_off: e.matmul(
                                ps, ARN[:, o + c * 512 + jj * 128:o + c * 512 + jj * 128 + 128],
                                ARN[:, lb + c * 512:lb + c * 512 + 512],
                                start=(c == 0), stop=(c == 7))
                            if c == 7:
                                mm_done = k.op(T, fn, inc="tS")
                            else:
                                k.op(T, fn)
                        base = Q0 if j < 8 else K0
                        cc = j if j < 8 else j - 8
                        doff = base + cc * 2048 + st_ * 512
                        k.wait(V, "tS", mm_done)
                        ev = k.op(V, lambda e, ps=ps, do=doff, j=j: e.tensor_scalar(
                            ARN[:, do:do + 512], ps, bqkv[:, j:j + 1], None,
                            OP.add, OP.bypass), inc="vS")
                        ps_free[pi] = ev
                else:
                    vg = grp - 4
                    for ti in range(4):
                        pi = rot % 6
                        rot += 1
                        ps = QKV_PS[pi]
                        k.wait(T, "vS", ps_free[pi])
                        for c in range(8):
                            k.op(T, lambda e, c=c, ti=ti, ps=ps, o=woff, lb=lnb_off: e.matmul(
                                ps, ARN[:, lb + c * 512 + ti * 128:
                                        lb + c * 512 + ti * 128 + 128],
                                ARN[:, o + c * 512:o + (c + 1) * 512],
                                start=(c == 0), stop=False))
                        k.wait(T, "dG", consts_dG)
                        mm_done = k.op(T, lambda e, vg=vg, ps=ps: e.matmul(
                            ps, onesb[0:1, 0:128], rowbank[0:1, vg * 512:(vg + 1) * 512],
                            start=False, stop=True), inc="tS")
                        jt = st_ * 4 + ti
                        voff = V0 + jt * 1040 + vg * 8 * 65
                        k.wait(V, "tS", mm_done)
                        ev = k.op(V, lambda e, ps=ps, vo=voff: e.tensor_copy(
                            ARN[:, vo:vo + 520].rearrange("p (h r) -> p h r", r=65)
                                [:, :, 0:64],
                            ps.rearrange("p (h dh) -> p h dh", dh=64)), inc="vS")
                        ps_free[pi] = ev
                wbuf_free[wpar] = mm_done
                strip_mm[st_] = mm_done
        k.op(V, lambda e: e.memset(ARN[:, V0:V0 + 16 * 1040].rearrange(
            "p (j h r) -> p j h r", h=16, r=65)[:, :, :, 64:65], 1.0), inc="vS")

        ra_off, rb_off = XS0, XS0 + 512
        psX_free = 0
        for base in ((Q0, K0) if _ROPE else ()):
            for c in range(8):
                for st_ in range(1, 4):
                    lo = max(PFX, st_ * 512)
                    hi = (st_ + 1) * 512
                    w = hi - lo
                    soff = base + c * 2048 + lo
                    coff = SC0 + NR + (lo - PFX)
                    sioff = SC0 + (lo - PFX)
                    va = k.op(V, lambda e, so=soff, co=coff, w=w: e.tensor_tensor(
                        ARN[:, ra_off:ra_off + w], ARN[:, so:so + w],
                        ARN[:, co:co + w], op=OP.mult), inc="vS")
                    k.wait(T, "vS", max(va, psX_free))
                    mm = k.op(T, lambda e, so=soff, w=w: e.matmul(
                        psX[:, 0:w], swapb[:], ARN[:, so:so + w], start=True, stop=True),
                        inc="tS")
                    k.wait(V, "tS", mm)
                    vpx = k.op(V, lambda e, si=sioff, w=w: e.tensor_tensor(
                        ARN[:, rb_off:rb_off + w], psX[:, 0:w],
                        ARN[:, si:si + w], op=OP.mult), inc="vS")
                    psX_free = vpx
                    k.op(V, lambda e, so=soff, w=w: e.tensor_tensor(
                        ARN[:, so:so + w], ARN[:, ra_off:ra_off + w],
                        ARN[:, rb_off:rb_off + w], op=OP.add), inc="vS")
        rope_done = k.cnt["vS"]

    if _PHASES >= 3:
        sq_off = XS0
        k.wait(SE, "vS", rope_done)
        for c in range(8):
            for which, base in ((0, Q0), (1, K0)):
                sv = k.op(SE, lambda e, b=base, c=c: e.activation(
                    out=ARN[:, sq_off:sq_off + 2048], in_=ARN[:, b + c * 2048:b + c * 2048 + 2048],
                    func=AF.Square), inc="sS")
                k.wait(T, "sS", sv)
                for st_ in range(4):
                    k.wait(T, "vS", psX_free)
                    mm = k.op(T, lambda e, st_=st_: e.matmul(
                        psX[0:1, 0:512], onesb[:, 0:1],
                        ARN[:, sq_off + st_ * 512:sq_off + (st_ + 1) * 512],
                        start=True, stop=True), inc="tS")
                    k.wait(V, "tS", mm)
                    if st_ == 0:
                        k.op(V, lambda e, w=which: e.tensor_scalar(
                            mrow[0:1, 0:512], psX[0:1, 0:512], 1.0, None, OP.mult,
                            OP.max, accum_out=misc[0:1, w:w + 1]), inc="vS")
                    else:
                        k.op(V, lambda e, w=which: e.tensor_scalar(
                            mrow[0:1, 0:512], psX[0:1, 0:512], 1.0, misc[0:1, w:w + 1],
                            OP.mult, OP.max, accum_out=misc[0:1, w:w + 1]), inc="vS")
                    k.wait(V, "vS")
                    psX_free = k.cnt["vS"]
            k.op(V, lambda e: e.tensor_tensor(misc[0:1, 2:3], misc[0:1, 0:1],
                                              misc[0:1, 1:2], op=OP.mult), inc="vS")
            k.wait(SE, "vS")
            ssv = k.op(SE, lambda e: e.activation(out=misc[0:1, 3:4], in_=misc[0:1, 2:3],
                                                  func=AF.Sqrt), inc="sS")
            k.wait(V, "sS", ssv)
            vneg = k.op(V, lambda e: e.tensor_scalar(miscb[0:1, 0:1], misc[0:1, 3:4],
                                                     -0.09375, None, OP.mult, OP.bypass),
                        inc="vS")
            vlnb = k.op(V, lambda e: e.tensor_scalar(misc[0:1, 5:6], misc[0:1, 3:4],
                                                     0.046875, None, OP.mult, OP.bypass),
                        inc="vS")
            k.wait(SE, "vS", vlnb)
            sbe = k.op(SE, lambda e: e.activation(out=misc[0:1, 6:7], in_=misc[0:1, 5:6],
                                                  func=AF.Exp), inc="sS")
            k.wait(T, "vS", max(vneg, psX_free))
            k.wait(T, "sS", sbe)
            k.wait(T, "gS", onesf_gS)
            mm = k.op(T, lambda e: e.matmul(psX[:, 0:1], onesb[0:1, 0:128],
                                            miscb[0:1, 0:1], start=True, stop=True), inc="tS")
            k.op(T, lambda e: e.matmul(psX[0:64, 1:2], onesf[0:1, 0:64],
                                       misc[0:1, 6:7], start=True, stop=True))
            mm2 = k.op(T, lambda e: e.matmul(psX[0:64, 2:3], onesf[0:1, 0:64],
                                             misc[0:1, 5:6], start=True, stop=True), inc="tS")
            k.wait(SE, "tS", mm2)
            sv2 = k.op(SE, lambda e, c=c: e.copy(out=misc[:, 16 + c:17 + c], in_=psX[:, 0:1]),
                       inc="sS")
            sv3 = k.op(SE, lambda e, c=c: e.copy(out=betatA[:, 2 * c:2 * c + 2],
                                                 in_=psX[0:64, 1:3]), inc="sS")
            psX_free_s = sv3
            k.wait(T, "sS", sv3)
        mprep_sS = k.cnt["sS"]
        mprep_vS = k.cnt["vS"]

    if _PHASES >= 4:
        PT0 = WS0
        k.wait(T, "vS", k.cnt["vS"])
        k.wait(T, "sS", k.cnt["sS"])
        state = {"psSS_free": 0, "pt_free": [0, 0], "psO_free": 0,
                 "psX_free2": 0, "otmp_free": 0, "pend": None, "pend_pv": None}
        wstate["fence"] = k.cnt["tS"]

        def emit_tail(c, st_, pv_cnt):
            hA, hB = 2 * c, 2 * c + 1
            k.wait(V, "tS", pv_cnt)
            rev = k.op(V, lambda e: e.tensor_scalar(
                r_sb[64:65, 0:1024], psO[64:65, 0:1024], 1.0, 1e-37,
                OP.mult, OP.max), inc="vS")
            um = 0
            for half in range(2):
                k.wait(T, "vS", rev)
                k.wait(T, "sS", state["psX_free2"])
                mmb = k.op(T, lambda e, half=half: e.matmul(
                    psX[0:64, 0:512], onesb[64:65, 0:64],
                    r_sb[64:65, half * 512:(half + 1) * 512], start=True, stop=True),
                    inc="tS")
                k.wait(SE, "tS", mmb)
                state["psX_free2"] = k.op(SE, lambda e, c2=2 * c: e.activation(
                    out=recipb[:, 0:512], in_=psX[0:64, 0:512], func=AF.Ln,
                    scale=betatA[:, c2:c2 + 1]), inc="sS")
                sxp = k.op(SE, lambda e, c2=2 * c: e.activation(
                    out=recip[:, 0:512], in_=recipb[:, 0:512], func=AF.Exp,
                    scale=-1.0, bias=betatA[:, c2 + 1:c2 + 2]), inc="sS")
                k.wait(V, "sS", sxp)
                if half == 0:
                    doff = Q0 + c * 2048 + st_ * 512
                    dst = ARN[0:64, doff:doff + 512]
                else:
                    k.wait(V, "dY", state["otmp_free"])
                    dst = otmp[0:64, 0:512]
                um = k.op(V, lambda e, d=dst, half=half: e.tensor_tensor(
                    d, psO[0:64, half * 512:(half + 1) * 512], recip[:, 0:512],
                    op=OP.mult), inc="vS")
            state["psO_free"] = um
            k.wait(Y, "vS", um)
            state["otmp_free"] = dY(lambda e, c=c, st_=st_: e.dma_start(
                out=ARN[64:128, Q0 + c * 2048 + st_ * 512:
                        Q0 + c * 2048 + st_ * 512 + 512],
                in_=otmp[0:64, 0:512]))

        for c in range(8):
            for st_ in range(4):
                lastpv = [None]

                def emit_pvpair(jp, ex, pto, c=c):
                    k.wait(T, "sS", ex)
                    if jp == 0:
                        k.wait(T, "vS", state["psO_free"])
                    pv = None
                    for jj in range(2):
                        j = 2 * jp + jj
                        for half in range(2):
                            h = 2 * c + half
                            fn = lambda e, h=h, j=j, jj=jj, half=half, po=pto: e.matmul(
                                psO[0:65, half * 512:(half + 1) * 512],
                                ARN[:, V0 + j * 1040 + h * 65:V0 + j * 1040 + h * 65 + 65],
                                ARN[:, po + jj * 1024 + half * 512:
                                    po + jj * 1024 + half * 512 + 512],
                                start=(j == 0), stop=(j == 15))
                            if jj == 1 and half == 1:
                                pv = k.op(T, fn, inc="tS")
                            else:
                                k.op(T, fn)
                    state["pt_free"][jp % 2] = pv
                    return pv

                for jp in range(8):
                    par = jp % 2
                    k.wait(T, "sS", state["psSS_free"])
                    mmS = None
                    for jj in range(2):
                        j = 2 * jp + jj
                        for half in range(2):
                            rw = half * 64
                            fn = lambda e, c=c, rw=rw, j=j, jj=jj, st_=st_, half=half: e.matmul(
                                psSS[:, jj * 1024 + half * 512:jj * 1024 + half * 512 + 512],
                                ARN[rw:rw + 64, K0 + c * 2048 + j * 128:
                                    K0 + c * 2048 + j * 128 + 128],
                                ARN[rw:rw + 64, Q0 + c * 2048 + st_ * 512:
                                    Q0 + c * 2048 + st_ * 512 + 512],
                                start=True, stop=True)
                            if jj == 1 and half == 1:
                                mmS = k.op(T, fn, inc="tS")
                            else:
                                k.op(T, fn)
                    if jp == 0 and state["pend_pv"] is not None:
                        state["pend_pv"]()
                        state["pend_pv"] = None
                    if jp == 1:
                        if state["pend"] is not None:
                            state["pend"]()
                            state["pend"] = None
                        lastpv[0]()
                    elif jp >= 2:
                        lastpv[0]()
                    pto = PT0 + par * 2048
                    k.wait(SE, "tS", max(mmS, state["pt_free"][par]))
                    ex = k.op(SE, lambda e, po=pto, bc=16 + c: e.activation(
                        out=ARN[:, po:po + 2048], in_=psSS[:], func=AF.Exp,
                        bias=misc[:, bc:bc + 1], scale=0.125), inc="sS")
                    state["psSS_free"] = ex
                    lastpv[0] = (lambda jp=jp, ex=ex, pto=pto, f=emit_pvpair: f(jp, ex, pto))
                    if jp == 3 or jp == 5:
                        emit_wjob()
                    if jp == 7 and (c * 4 + st_) % 8 == 7:
                        emit_wjob()
                state["pend_pv"] = lastpv[0]

                def mk_tail(c=c, st_=st_):
                    def f():
                        emit_tail(c, st_, k.cnt["tS"])
                    return f
                state["pend"] = mk_tail()
        if state["pend_pv"] is not None:
            state["pend_pv"]()
            state["pend_pv"] = None
        if state["pend"] is not None:
            state["pend"]()
            state["pend"] = None
        while wstate["i"] < len(wjobs):
            emit_wjob()
        w_store_dY = max(wstate["st"])
        s_done_dY = k.cnt["dY"]
        s_done_vS = k.cnt["vS"]
        s_done_sS = k.cnt["sS"]

    if _PHASES >= 5:
        k.wait(G, "sS", s_done_sS)
        k.wait(G, "tS", k.cnt["tS"])
        ldp = None
        for c in range(8):
            ldp = dG(lambda e, c=c: e.dma_start(
                out=ARN[:, WS0 + c * 1024:WS0 + (c + 1) * 1024],
                in_=wproj_h[c * 128:(c + 1) * 128, :]))
        k.wait(T, "dG", ldp)
        k.wait(T, "dY", s_done_dY)
        k.wait(T, "vS", s_done_vS)
        k.wait(T, "sS", s_done_sS)
        PPS = (psSS[:, 0:512], psSS[:, 512:1024], psSS[:, 1024:1536], psSS[:, 1536:2048])
        prot = 0
        pfree = [0] * 4
        x2ev = 0
        lastvln2 = [0, 0]
        for qt in range(16):
            par = qt % 2
            xv = xt[:, par * D:par * D + D]
            if qt >= 2:
                k.wait(Y, "vS", lastvln2[par])
            else:
                k.wait(Y, "vS", lastvln[par])
            dx = dY(lambda e, x=xv, qt=qt: e.dma_start(out=x, in_=x_h[qt * 128:(qt + 1) * 128, :]))
            for s2 in range(2):
                pi = prot % 4
                prot += 1
                ps = PPS[pi]
                k.wait(T, "vS", pfree[pi])
                for c in range(8):
                    k.op(T, lambda e, c=c, qt=qt, s2=s2, ps=ps: e.matmul(
                        ps, ARN[:, Q0 + c * 2048 + qt * 128:Q0 + c * 2048 + qt * 128 + 128],
                        ARN[:, WS0 + c * 1024 + s2 * 512:WS0 + c * 1024 + s2 * 512 + 512],
                        start=(c == 0), stop=False))
                mmp = k.op(T, lambda e, s2=s2, ps=ps: e.matmul(
                    ps, onesb[0:1, 0:128], rowbank[0:1, 1024 + s2 * 512:1536 + s2 * 512],
                    start=False, stop=True), inc="tS")
                k.wait(V, "tS", mmp)
                k.wait(V, "dY", dx)
                x2ev = k.op(V, lambda e, ps=ps, x=xv, s2=s2: e.tensor_tensor(
                    x[:, s2 * 512:(s2 + 1) * 512], ps, x[:, s2 * 512:(s2 + 1) * 512],
                    op=OP.add), inc="vS")
                pfree[pi] = x2ev
            k.wait(Y, "vS", x2ev)
            dY(lambda e, x=xv, qt=qt: e.dma_start(out=x2_d[qt * 128:(qt + 1) * 128, :], in_=x))
            k.op(V, lambda e, x=xv: e.bn_stats(stats[:, 0:6], x[:, 0:512]), inc="vS")
            k.wait(V, "vS")
            k.op(V, lambda e, x=xv: e.bn_stats(stats[:, 6:12], x[:, 512:1024]), inc="vS")
            k.wait(V, "vS")
            k.op(V, lambda e: e.bn_aggr(mv[:], stats[:]), inc="vS")
            k.wait(V, "vS")
            veps = k.op(V, lambda e: e.tensor_scalar(mv[:, 1:2], mv[:, 1:2], 1e-5, None,
                                                     OP.add, OP.bypass), inc="vS")
            k.wait(SE, "vS", veps)
            ssq = k.op(SE, lambda e: e.activation(out=rstd[:, 0:1], in_=mv[:, 1:2],
                                                  func=AF.Sqrt), inc="sS")
            k.wait(V, "sS", ssq)
            k.op(V, lambda e: e.reciprocal(rstd[:, 1:2], rstd[:, 0:1]), inc="vS")
            k.wait(V, "vS")
            ln2 = ARNF[:, K0 // 2:K0 // 2 + 1024]
            ln2q = ARN[:, K0 + 2048:K0 + 3072]
            k.wait(V, "sS", qtrack["s"])
            vln2 = k.op(V, lambda e, x=xv, o=ln2: e.tensor_scalar(
                o, x, mv[:, 0:1], rstd[:, 1:2], OP.subtract, OP.mult), inc="vS")
            lastvln2[par] = vln2
            vq, _, _ = quant(ln2, ln2q, 1024, src_v=vln2)
            k.wait(Y, "vS", vq)
            dY(lambda e, q=ln2q, qt=qt: e.dma_start(
                out=ln2q_d[qt * 128:(qt + 1) * 128, :], in_=q))
        p_done_dY = k.cnt["dY"]
        f0_done_dY = k.cnt["dY"]

    if _PHASES >= 7:
        k.wait(Y, "dY", f0_done_dY)
        k.wait(Y, "vS", k.cnt["vS"])
        k.wait(Y, "sS", k.cnt["sS"])
        ldw = None
        for c in range(8):
            ldw = dY(lambda e, c=c: e.dma_start(
                out=ARN[:, FG0 + c * 2816:FG0 + (c + 1) * 2816],
                in_=wgq_d[:, c * 128:(c + 1) * 128], transpose=True))
        for c in range(8):
            ldw = dY(lambda e, c=c: e.dma_start(
                out=ARN[:, FU0 + c * 2816:FU0 + (c + 1) * 2816],
                in_=wuq_d[:, c * 128:(c + 1) * 128], transpose=True))
        ISTR = [(0, 512), (512, 1024), (1024, 1536), (1536, 2048), (2048, 2560),
                (2560, 2816)]
        k.wait(T, "vS", k.cnt["vS"])
        k.wait(T, "sS", k.cnt["sS"])
        FPS = (psSS[:, 0:512], psSS[:, 512:1024], psSS[:, 1024:1536], psSS[:, 1536:2048])
        frot = 0
        ffree = [(None, 0)] * 4
        lnT_used = {}
        hq_store = {}
        hbuf = ARN[:, FH0:FH0 + 2816]
        sgb = ARN[:, FH0 + 2816:FH0 + 5632]
        hqb = ARN[:, FH0 + 5632:FH0 + 8448]
        for g in range(4):
            lpar = g % 2
            lnT_off = FL0 + lpar * 4096
            if g >= 2:
                k.wait(Y, "tS", lnT_used[g - 2])
            ldt = None
            for c in range(8):
                ldt = dY(lambda e, c=c, g=g, o=lnT_off: e.dma_start(
                    out=ARN[:, o + c * 512:o + (c + 1) * 512],
                    in_=ln2q_d[g * 512:(g + 1) * 512, c * 128:(c + 1) * 128], transpose=True))
            k.wait(T, "dY", ldt)
            if g == 0:
                k.wait(T, "dY", ldw)
            for ti in range(4):
                tt_ = g * 4 + ti
                if tt_ >= 1:
                    k.wait(V, "dY", hq_store[tt_ - 1])
                mmu = None
                for (i0, i1) in ISTR:
                    w = i1 - i0
                    pi = frot % 4
                    frot += 1
                    ps = FPS[pi]
                    snm, scnt = ffree[pi]
                    if scnt:
                        k.wait(T, snm, scnt)
                    for c in range(8):
                        fn = lambda e, c=c, ti=ti, i0=i0, w=w, ps=ps, o=lnT_off: e.matmul(
                            ps[:, 0:w], ARN[:, o + c * 512 + ti * 128:o + c * 512 + ti * 128 + 128],
                            ARN[:, FG0 + c * 2816 + i0:FG0 + c * 2816 + i0 + w],
                            start=(c == 0), stop=(c == 7))
                        if c == 7:
                            mmg = k.op(T, fn, inc="tS")
                        else:
                            k.op(T, fn)
                    k.wait(SE, "tS", mmg)
                    k.wait(SE, "vS", qtrack["v"])
                    sgd = k.op(SE, lambda e, ps=ps, i0=i0, w=w: e.activation(
                        out=sgb[:, i0:i0 + w], in_=ps[:, 0:w], func=AF.Silu), inc="sS")
                    ffree[pi] = ("sS", sgd)
                    pi = frot % 4
                    frot += 1
                    ps = FPS[pi]
                    snm, scnt = ffree[pi]
                    if scnt:
                        k.wait(T, snm, scnt)
                    for c in range(8):
                        fn = lambda e, c=c, ti=ti, i0=i0, w=w, ps=ps, o=lnT_off: e.matmul(
                            ps[:, 0:w], ARN[:, o + c * 512 + ti * 128:o + c * 512 + ti * 128 + 128],
                            ARN[:, FU0 + c * 2816 + i0:FU0 + c * 2816 + i0 + w],
                            start=(c == 0), stop=(c == 7))
                        if c == 7:
                            mmu = k.op(T, fn, inc="tS")
                        else:
                            k.op(T, fn)
                    k.wait(V, "tS", mmu)
                    k.wait(V, "sS", sgd)
                    hdone = k.op(V, lambda e, ps=ps, i0=i0, w=w: e.tensor_tensor(
                        hbuf[:, i0:i0 + w], sgb[:, i0:i0 + w], ps[:, 0:w], op=OP.mult),
                        inc="vS")
                    ffree[pi] = ("vS", hdone)
                lnT_used[g] = mmu
                hfin = k.cnt["vS"]
                quant(hbuf[:, 0:1408], hqb[:, 0:1408], 1408, src_v=hfin)
                vq, _, _ = quant(hbuf[:, 1408:2816], hqb[:, 1408:2816], 1408, src_v=hfin)
                k.wait(Y, "vS", vq)
                hq_store[tt_] = dY(lambda e, q=hqb, tt_=tt_: e.dma_start(
                    out=hq_d[tt_ * 128:(tt_ + 1) * 128, :], in_=q))
        f1_done_dY = k.cnt["dY"]

    if _PHASES >= 8:
        k.wait(Y, "dY", f1_done_dY)
        k.wait(Y, "tS", k.cnt["tS"])
        ldwd = None
        for i in range(22):
            ldwd = dY(lambda e, i=i: e.dma_start(
                out=ARN[:, FU0 + i * 1024:FU0 + (i + 1) * 1024],
                in_=wdq_d[:, i * 128:(i + 1) * 128], transpose=True))
        k.wait(T, "vS", k.cnt["vS"])
        k.wait(T, "sS", k.cnt["sS"])
        hT_used = {}
        out_dma = [0, 0]
        oev_half = [0, 0]
        lastout = [0, 0]
        for g in range(4):
            hpar = g % 2
            hT_off = FG0 + hpar * 11264
            if g >= 2:
                k.wait(Y, "tS", hT_used[g - 2])
            ldh = None
            for i in range(22):
                ldh = dY(lambda e, i=i, g=g, o=hT_off: e.dma_start(
                    out=ARN[:, o + i * 512:o + (i + 1) * 512],
                    in_=hq_d[g * 512:(g + 1) * 512, i * 128:(i + 1) * 128], transpose=True))
            k.wait(T, "dY", ldh)
            if g == 0:
                k.wait(T, "dY", ldwd)
            for ti in range(4):
                tt_ = g * 4 + ti
                par = tt_ % 2
                xv = xt[:, par * D:par * D + D]
                ov = ot[:, par * D:par * D + D]
                if tt_ >= 2:
                    k.wait(Y, "dY", out_dma[par])
                    k.wait(Y, "vS", lastout[par])
                dx = dY(lambda e, x=xv, tt_=tt_: e.dma_start(
                    out=x, in_=x2_d[tt_ * 128:(tt_ + 1) * 128, :]))
                mmd = None
                oev = 0
                for s2 in range(2):
                    ps = psSS[:, 0:512] if s2 == 0 else psSS[:, 512:1024]
                    k.wait(T, "vS", oev_half[s2])
                    for i in range(22):
                        fn = lambda e, i=i, ti=ti, s2=s2, ps=ps, o=hT_off: e.matmul(
                            ps, ARN[:, o + i * 512 + ti * 128:o + i * 512 + ti * 128 + 128],
                            ARN[:, FU0 + i * 1024 + s2 * 512:FU0 + i * 1024 + s2 * 512 + 512],
                            start=(i == 0), stop=(i == 21))
                        if i == 21:
                            mmd = k.op(T, fn, inc="tS")
                        else:
                            k.op(T, fn)
                    k.wait(V, "tS", mmd)
                    k.wait(V, "dY", dx)
                    oev = k.op(V, lambda e, ps=ps, x=xv, o=ov, s2=s2: e.tensor_tensor(
                        o[:, s2 * 512:(s2 + 1) * 512], ps, x[:, s2 * 512:(s2 + 1) * 512],
                        op=OP.add), inc="vS")
                    oev_half[s2] = oev
                hT_used[g] = mmd
                lastout[par] = oev
                k.wait(Y, "vS", oev)
                out_dma[par] = dY(lambda e, o=ov, tt_=tt_: e.dma_start(
                    out=out_h[tt_ * 128:(tt_ + 1) * 128, :], in_=o))
        k.wait(Y, "dY", k.cnt["dY"])

    if _DEBUG:
        k.wait(Y, "vS", k.cnt["vS"])
        k.wait(Y, "sS", k.cnt["sS"])
        k.wait(Y, "gS", k.cnt["gS"])
        k.wait(Y, "tS", k.cnt["tS"])
        k.wait(Y, "dG", k.cnt["dG"])
        k.wait(Y, "dY", k.cnt["dY"])
        if "wq" in _DEBUG:
            for nm, ap, shp in (("d_wgq", wgq_d, [DI, D]), ("d_wuq", wuq_d, [DI, D]),
                                ("d_wdq", wdq_d, [D, DI])):
                o = nc.dram_tensor(nm, shp, BF16, kind="ExternalOutput").ap()
                dY(lambda e, o=o, a=ap: e.dma_start(out=o, in_=a))
        if "ln1" in _DEBUG:
            o = nc.dram_tensor("d_ln1", [128, 8192], BF16, kind="ExternalOutput").ap()
            dY(lambda e, o=o: e.dma_start(out=o, in_=ARN[:, LN0:LN0 + 8192]))
        if "qkv" in _DEBUG:
            for nm, off, w in (("d_q", Q0, 16384), ("d_k", K0, 16384), ("d_v", V0, 16640)):
                o = nc.dram_tensor(nm, [128, w], BF16, kind="ExternalOutput").ap()
                dY(lambda e, o=o, off=off, w=w: e.dma_start(out=o, in_=ARN[:, off:off + w]))
        if "att" in _DEBUG:
            o = nc.dram_tensor("d_att", [128, 16384], BF16, kind="ExternalOutput").ap()
            dY(lambda e, o=o: e.dma_start(out=o, in_=ARN[:, Q0:Q0 + 16384]))
        if "x2" in _DEBUG:
            o = nc.dram_tensor("d_x2", [S, D], F32, kind="ExternalOutput").ap()
            dY(lambda e, o=o: e.dma_start(out=o, in_=x2_d))
        if "ln2q" in _DEBUG:
            o = nc.dram_tensor("d_ln2q", [S, D], BF16, kind="ExternalOutput").ap()
            dY(lambda e, o=o: e.dma_start(out=o, in_=ln2q_d))
        if "hq" in _DEBUG:
            o = nc.dram_tensor("d_hq", [S, DI], BF16, kind="ExternalOutput").ap()
            dY(lambda e, o=o: e.dma_start(out=o, in_=hq_d))
        k.wait(Y, "dY", k.cnt["dY"])

    with nc.Block() as block:
        @block.tensor
        def _(eng):
            for fn in k.prog["T"]:
                fn(eng)

        @block.vector
        def _(eng):
            for fn in k.prog["V"]:
                fn(eng)

        @block.scalar
        def _(eng):
            for fn in k.prog["S"]:
                fn(eng)

        @block.gpsimd
        def _(eng):
            for fn in k.prog["G"]:
                fn(eng)

        @block.sync
        def _(eng):
            for fn in k.prog["Y"]:
                fn(eng)

    stk.close()
    return nc


def _host_inputs(full):
    x = np.ascontiguousarray(full["L0_attn_x"], dtype=np.float32)
    sinT = np.ascontiguousarray(np.asarray(full["L0_attn_sin"], dtype=np.float32).T)
    cosT = np.ascontiguousarray(np.asarray(full["L0_attn_cos"], dtype=np.float32).T)
    wqkv = np.ascontiguousarray(full["L0_attn_w_qkv"], dtype=np.float32)
    b_qkv = np.asarray(full["L0_attn_b_qkv"], dtype=np.float32)
    bqkv = np.ascontiguousarray(b_qkv.reshape(24, 128).T)
    brows = np.ascontiguousarray(
        np.concatenate([b_qkv[2048:3072],
                        np.asarray(full["L0_attn_b_proj"], dtype=np.float32)]).reshape(1, 2048))
    wproj = np.ascontiguousarray(full["L0_attn_w_proj"], dtype=np.float32)
    wg = np.ascontiguousarray(full["L0_ffn_w_gate"], dtype=np.float32)
    wu = np.ascontiguousarray(full["L0_ffn_w_up"], dtype=np.float32)
    wd = np.ascontiguousarray(full["L0_ffn_w_down"], dtype=np.float32)
    ones = np.ones((128, 128), dtype=ml_dtypes.bfloat16)
    ident = np.eye(128, dtype=ml_dtypes.bfloat16)
    swap = np.zeros((128, 128), dtype=np.float32)
    for blk in range(2):
        b = blk * 64
        for m in range(32):
            swap[b + m + 32, b + m] = -1.0
            swap[b + m, b + m + 32] = 1.0
    swap = swap.astype(ml_dtypes.bfloat16)
    smaskv = np.full((128, 1), -32768, dtype=np.int16)
    maps = []
    for c in range(NCORES):
        maps.append({
            "x": np.ascontiguousarray(x[c]),
            "sinT": sinT, "cosT": cosT, "wqkv": wqkv, "bqkv_pp": bqkv,
            "brows": brows, "wproj": wproj, "wg": wg, "wu": wu, "wd": wd,
            "ones": ones, "ident": ident, "swap": swap, "smask": smaskv,
        })
    return maps


def kernel(**inputs):
    maps = _host_inputs(inputs)
    nc = build_nc()
    res = run_bass_kernel_spmd(nc, maps, core_ids=list(range(NCORES)))
    out = np.stack([res.results[c]["out"] for c in range(NCORES)], axis=0)
    return np.ascontiguousarray(out, dtype=np.float32)


# version 28 (final)
# speedup vs baseline: 1.1769x; 1.1769x over previous
import numpy as np
import ml_dtypes
import concourse.bass as bass
import concourse.mybir as mybir
from concourse.bass_utils import run_bass_kernel_spmd

F32 = mybir.dt.float32
BF16 = mybir.dt.bfloat16
I16 = mybir.dt.int16
AF = mybir.ActivationFunctionType
OP = mybir.AluOpType
AX = mybir.AxisListType

S = 2048
D = 1024
NH = 16
DH = 64
NR = 1369
PFX = S - NR
DI = 2816
NCORES = 8
CMAG = 1.5 * 2 ** 23

ENGS = ("T", "V", "S", "G", "Y")
_PHASES = 99
_DEBUG = ()
_LSTRIPS = 4
_ROPE = True


class Sched:
    def __init__(self, sems):
        self.prog = {e: [] for e in ENGS}
        self.cnt = {n: 0 for n in sems}
        self.sems = sems

    def emit(self, eng, fn):
        self.prog[eng].append(fn)

    def wait(self, eng, name, val=None):
        v = self.cnt[name] if val is None else val
        if not v or v <= 0:
            return
        self.emit(eng, lambda e, s=self.sems[name], v=v: e.wait_ge(s, v))

    def op(self, eng, fn, inc=None, n=1):
        if inc is not None:
            self.cnt[inc] += n
            self.emit(eng, lambda e, f=fn, s=self.sems[inc], n=n: f(e).then_inc(s, n))
            return self.cnt[inc]
        self.emit(eng, lambda e, f=fn: f(e))
        return None


def build_nc():
    nc = bass.Bass("TRN2", target_bir_lowering=False, debug=False)

    x_h = nc.dram_tensor("x", [S, D], F32, kind="ExternalInput").ap()
    sinT_h = nc.dram_tensor("sinT", [DH, NR], F32, kind="ExternalInput").ap()
    cosT_h = nc.dram_tensor("cosT", [DH, NR], F32, kind="ExternalInput").ap()
    wqkv_h = nc.dram_tensor("wqkv", [D, 3 * D], F32, kind="ExternalInput").ap()
    bqkv_h = nc.dram_tensor("bqkv_pp", [128, 24], F32, kind="ExternalInput").ap()
    brows_h = nc.dram_tensor("brows", [1, 2048], F32, kind="ExternalInput").ap()
    wproj_h = nc.dram_tensor("wproj", [D, D], F32, kind="ExternalInput").ap()
    wg_h = nc.dram_tensor("wg", [DI, D], F32, kind="ExternalInput").ap()
    wu_h = nc.dram_tensor("wu", [DI, D], F32, kind="ExternalInput").ap()
    wd_h = nc.dram_tensor("wd", [D, DI], F32, kind="ExternalInput").ap()
    ones_h = nc.dram_tensor("ones", [128, 128], BF16, kind="ExternalInput").ap()
    ident_h = nc.dram_tensor("ident", [128, 128], BF16, kind="ExternalInput").ap()
    swap_h = nc.dram_tensor("swap", [128, 128], BF16, kind="ExternalInput").ap()
    smask_h = nc.dram_tensor("smask", [128, 1], I16, kind="ExternalInput").ap()
    out_h = nc.dram_tensor("out", [S, D], F32, kind="ExternalOutput").ap()
    x2_d = nc.dram_tensor("x2_d", [S, D], F32, kind="Internal").ap()
    ln2q_d = nc.dram_tensor("ln2q_d", [S, D], BF16, kind="Internal").ap()
    hq_d = nc.dram_tensor("hq_d", [S, DI], BF16, kind="Internal").ap()
    wgq_d = nc.dram_tensor("wgq_d", [DI, D], BF16, kind="Internal").ap()
    wuq_d = nc.dram_tensor("wuq_d", [DI, D], BF16, kind="Internal").ap()
    wdq_d = nc.dram_tensor("wdq_d", [D, DI], BF16, kind="Internal").ap()

    from contextlib import ExitStack
    stk = ExitStack()
    sb = lambda n, s, d: stk.enter_context(nc.sbuf_tensor(n, s, d))
    psa = lambda n, s, d: stk.enter_context(nc.psum_tensor(n, s, d))

    ARN_COLS = 72960
    ARN = sb("arena", [128, ARN_COLS], BF16)
    Q0 = 0
    K0 = 16384
    V0 = 32768
    SC0 = 49408
    WS0 = 52146
    LN0 = 60338
    XS0 = 68530
    FG0 = 0
    FU0 = 22528
    FL0 = 45056
    FH0 = 53248
    QS0 = FH0 + 8448


    rowbank = sb("rowbank", [128, 2048], BF16)
    onesb = sb("onesb", [128, 128], BF16)
    identb = sb("identb", [128, 128], BF16)
    swapb = sb("swapb", [128, 128], BF16)
    smask = sb("smaskb", [128, 1], I16)
    bqkv = sb("bqkvb", [128, 24], F32)
    miscb = sb("miscbf", [128, 8], BF16)
    xt = sb("xt", [128, 3 * D], F32)
    ot = sb("ot", [128, 2 * D], F32)
    dd = sb("ddt", [128, 1408], F32)
    stats = sb("stats", [128, 12], F32)
    mv = sb("mv", [128, 2], F32)
    rstd = sb("rstdt", [128, 2], F32)
    bm = sb("bmt", [128, 176], F32)
    rcp = sb("rcpt", [128, 176], F32)
    rs2 = sb("rs2t", [128, 176], F32)
    sc2 = sb("sc2t", [128, 176], F32)
    sgn = sb("sgnt", [128, 1408], I16)
    sgnb = sb("sgnbt", [128, 1408], BF16)
    misc = sb("miscf", [128, 64], F32)
    mrow = sb("mrowf", [1, 512], F32)
    r_sb = sb("r_sb", [65, 1024], BF16)
    recip = sb("recipt", [64, 512], F32)
    recipb = sb("recipbt", [64, 512], F32)
    betat = sb("betat", [64, 2], F32)
    betatA = sb("betatA", [64, 16], F32)
    onesf = sb("onesft", [128, 64], F32)
    otmp = sb("otmpt", [64, 1024], BF16)

    psSS = psa("psSS", [128, 2048], F32)
    psO = psa("psO", [128, 1024], F32)
    psX = psa("psX", [128, 512], F32)
    psT = psa("psT", [128, 1024], BF16)

    sems = {}
    for name in ("tS", "vS", "sS", "gS", "dY", "dG"):
        sems[name] = stk.enter_context(nc.semaphore(name))
    k = Sched(sems)
    T, V, SE, G, Y = ENGS

    def dY(fn):
        return k.op(Y, fn, inc="dY", n=16)

    def dG(fn):
        return k.op(G, fn, inc="dG", n=16)

    dY(lambda e: e.dma_start(out=onesb[:], in_=ones_h))
    dY(lambda e: e.dma_start(out=identb[:], in_=ident_h))
    dY(lambda e: e.dma_start(out=swapb[:], in_=swap_h))
    dY(lambda e: e.dma_start(out=smask[:], in_=smask_h))
    dY(lambda e: e.dma_start(out=bqkv[:], in_=bqkv_h))
    dG(lambda e: e.dma_start(out=rowbank[0:1, 0:2048], in_=brows_h))
    dG(lambda e: e.dma_start(out=ARN[0:64, SC0:SC0 + NR], in_=sinT_h))
    dG(lambda e: e.dma_start(out=ARN[64:128, SC0:SC0 + NR], in_=sinT_h))
    dG(lambda e: e.dma_start(out=ARN[0:64, SC0 + NR:SC0 + 2 * NR], in_=cosT_h))
    consts_dG = dG(lambda e: e.dma_start(out=ARN[64:128, SC0 + NR:SC0 + 2 * NR], in_=cosT_h))
    consts_dY = k.cnt["dY"]
    k.op(G, lambda e: e.memset(onesf[:], 1.0), inc="gS")
    onesf_gS = k.cnt["gS"]

    ARNF = ARN[:].bitcast(F32)
    QSC = [QS0 + i * 2816 for i in range(4)]
    qtrack = {"v": 0, "s": 0, "g": 0}

    def quant(src, dst, F, src_v=0, qsc=None):
        sc_ = QSC if qsc is None else qsc
        t_ = ARNF[:, sc_[0] // 2:sc_[0] // 2 + F]
        b1 = ARNF[:, sc_[1] // 2:sc_[1] // 2 + F]
        b2 = ARNF[:, sc_[2] // 2:sc_[2] // 2 + F]
        b3 = ARNF[:, sc_[3] // 2:sc_[3] // 2 + F]
        nb = F // 16
        bmv, rcpv, rs2v, sc2v = bm[:, 0:nb], rcp[:, 0:nb], rs2[:, 0:nb], sc2[:, 0:nb]
        sgv, ddv = sgn[:, 0:F], dd[:, 0:F]
        zb = sgnb[:, 0:F]
        k.wait(V, "gS", qtrack["g"])
        k.wait(V, "sS", qtrack["s"])
        k.op(V, lambda e: e.tensor_reduce(bmv, src.rearrange("p (nb b) -> p nb b", b=16),
                                          axis=AX.X, op=OP.max, apply_absolute_value=True),
             inc="vS")
        k.wait(V, "vS")
        k.op(V, lambda e: e.reciprocal(rcpv, bmv), inc="vS")
        k.wait(V, "vS")
        k.op(V, lambda e: e.tensor_scalar(rs2v, rcpv, 6.0, 6.0e12, OP.mult, OP.min))
        k.op(V, lambda e: e.tensor_scalar(sc2v, bmv, 1.0 / 6.0, 1e-12, OP.mult, OP.max),
             inc="vS")
        k.wait(V, "vS")
        k.op(V, lambda e: e.tensor_copy(zb, src))
        k.wait(SE, "gS", qtrack["g"])
        k.wait(SE, "vS", max(qtrack["v"], src_v))
        s_abs = k.op(SE, lambda e: e.activation(out=b1, in_=src, func=AF.Abs), inc="sS")
        k.wait(V, "sS", s_abs)
        rsb = rs2v.unsqueeze(-1).broadcast_to([128, nb, 16])
        vt = k.op(V, lambda e: e.tensor_tensor(t_.rearrange("p (nb b) -> p nb b", b=16),
                                               b1.rearrange("p (nb b) -> p nb b", b=16),
                                               rsb, op=OP.mult), inc="vS")
        k.wait(G, "vS", vt)
        k.op(G, lambda e: e.tensor_scalar(b1, t_, 2.0, 4.0, OP.mult, OP.add))
        k.op(G, lambda e: e.tensor_scalar(b2, t_, 8.0, 1.0, OP.add, OP.mult))
        gd1 = k.op(G, lambda e: e.tensor_scalar(b3, t_, 4.0, 1.0, OP.mult, OP.mult), inc="gS")
        k.wait(V, "gS", gd1)
        k.op(V, lambda e: e.tensor_tensor(b2, b2, b1, op=OP.min))
        vmin = k.op(V, lambda e: e.tensor_tensor(b3, b3, b2, op=OP.min), inc="vS")
        k.wait(SE, "vS", vmin)
        k.op(SE, lambda e: e.activation(out=ddv, in_=b3, func=AF.Copy, bias=CMAG, scale=0.5))
        vbf = ARN[:, sc_[2]:sc_[2] + F]
        g1b = ARN[:, sc_[2] + F:sc_[2] + 2 * F]
        h1b = ARN[:, sc_[1]:sc_[1] + F]
        h2b = ARN[:, sc_[1] + F:sc_[1] + 2 * F]
        s_v = k.op(SE, lambda e: e.activation(out=vbf, in_=ddv, func=AF.Copy,
                                              bias=-CMAG / 2, scale=0.5), inc="sS")
        k.wait(G, "sS", s_v)
        k.op(G, lambda e: e.tensor_scalar(h1b, vbf, 2.0, -2.0, OP.mult, OP.add))
        g_done = k.op(G, lambda e: e.tensor_scalar(h2b, vbf, 4.0, -8.0, OP.mult, OP.add),
                      inc="gS")
        k.wait(V, "gS", g_done)
        k.op(V, lambda e: e.tensor_tensor(g1b, vbf, h1b, op=OP.max))
        k.op(V, lambda e: e.tensor_tensor(h1b, g1b, h2b, op=OP.max))
        scb = sc2v.unsqueeze(-1).broadcast_to([128, nb, 16])
        k.op(V, lambda e: e.tensor_tensor(dst.rearrange("p (nb b) -> p nb b", b=16),
                                          h1b.rearrange("p (nb b) -> p nb b", b=16),
                                          scb, op=OP.mult))
        k.op(V, lambda e: e.tensor_scalar(sgv, zb.bitcast(I16), smask[:, 0:1], None,
                                          OP.bitwise_and, OP.bypass))
        v_done = k.op(V, lambda e: e.tensor_tensor(dst.bitcast(I16), dst.bitcast(I16), sgv,
                                                   op=OP.bitwise_or), inc="vS")
        qtrack.update(v=v_done, s=s_v, g=g_done)
        return v_done, s_abs, g_done

    wjobs = []
    for i in range(22):
        wjobs.append((wg_h[i * 128:(i + 1) * 128, :], wgq_d[i * 128:(i + 1) * 128, :], 1024))
    for i in range(22):
        wjobs.append((wu_h[i * 128:(i + 1) * 128, :], wuq_d[i * 128:(i + 1) * 128, :], 1024))
    for i in range(8):
        for c0, c1 in ((0, 1024), (1024, 2048), (2048, 2816)):
            wjobs.append((wd_h[i * 128:(i + 1) * 128, c0:c1],
                          wdq_d[i * 128:(i + 1) * 128, c0:c1], c1 - c0))
    WQS = [SC0, LN0, LN0 + 2048, LN0 + 4096]
    WRAW = [LN0 + 6144, LN0 + 8192]
    WQO = [LN0 + 10240, LN0 + 11264]
    wstate = {"i": 0, "ld": [0, 0], "q": [None, None], "st": [0, 0], "fence": 0}

    def emit_wjob():
        i = wstate["i"]
        if i >= len(wjobs):
            return
        if i == 0:
            fence = wstate["fence"]
            k.wait(G, "tS", fence)
            k.wait(V, "tS", fence)
            k.wait(SE, "tS", fence)
        wstate["i"] = i + 1
        srcap, dstap, F = wjobs[i]
        par = i % 2
        raw = ARNF[:, WRAW[par] // 2:WRAW[par] // 2 + F]
        qout = ARN[:, WQO[par]:WQO[par] + F]
        if wstate["q"][par] is not None:
            vd, sa, gd = wstate["q"][par]
            k.wait(G, "vS", vd)
            k.wait(G, "sS", sa)
        ld = dG(lambda e, r=raw, s=srcap: e.dma_start(out=r, in_=s))
        k.wait(V, "dG", ld)
        k.wait(SE, "dG", ld)
        if wstate["st"][par]:
            k.wait(V, "dY", wstate["st"][par])
        wstate["q"][par] = quant(raw, qout, F, qsc=WQS)
        k.wait(Y, "vS", wstate["q"][par][0])
        wstate["st"][par] = dY(lambda e, q=qout, dst=dstap: e.dma_start(out=dst, in_=q))

    if _PHASES >= 2:
        k.wait(T, "dY", consts_dY)
        k.wait(T, "dG", consts_dG)
        lastvln = [0, 0]
        tr_done = 0
        psT_evac = 0
        ln1_done = [0] * 4
        strip_mm = [0] * 4
        wbuf_free = [0, 0]
        QKV_PS = (psSS[:, 0:512], psSS[:, 512:1024], psSS[:, 1024:1536],
                  psSS[:, 1536:2048], psO[:, 0:512], psO[:, 512:1024])
        rot = 0
        ps_free = [0] * 6
        for st_ in range(_LSTRIPS):
            lnb_off = LN0 + (st_ % 2) * 4096
            for ti in range(4):
                g_ti = st_ * 4 + ti
                par = g_ti % 2
                t0 = st_ * 512 + ti * 128
                xv = xt[:, par * D:par * D + D]
                if g_ti >= 2:
                    k.wait(Y, "vS", lastvln[par])
                dx = dY(lambda e, x=xv, t0=t0: e.dma_start(out=x, in_=x_h[t0:t0 + 128, :]))
                k.wait(V, "dY", dx)
                k.op(V, lambda e, x=xv: e.bn_stats(stats[:, 0:6], x[:, 0:512]), inc="vS")
                k.wait(V, "vS")
                k.op(V, lambda e, x=xv: e.bn_stats(stats[:, 6:12], x[:, 512:1024]), inc="vS")
                k.wait(V, "vS")
                k.op(V, lambda e: e.bn_aggr(mv[:], stats[:]), inc="vS")
                k.wait(V, "vS")
                veps = k.op(V, lambda e: e.tensor_scalar(mv[:, 1:2], mv[:, 1:2], 1e-5, None,
                                                         OP.add, OP.bypass), inc="vS")
                k.wait(SE, "vS", veps)
                ssq = k.op(SE, lambda e: e.activation(out=rstd[:, 0:1], in_=mv[:, 1:2],
                                                      func=AF.Sqrt), inc="sS")
                k.wait(V, "sS", ssq)
                k.op(V, lambda e: e.reciprocal(rstd[:, 1:2], rstd[:, 0:1]), inc="vS")
                k.wait(V, "vS")
                lnt = ARN[:, XS0:XS0 + 1024]
                if g_ti >= 1:
                    k.wait(V, "tS", tr_done)
                vln = k.op(V, lambda e, x=xv, o=lnt: e.tensor_scalar(
                    o, x, mv[:, 0:1], rstd[:, 1:2], OP.subtract, OP.mult), inc="vS")
                lastvln[par] = vln
                k.wait(T, "vS", vln)
                if g_ti >= 1:
                    k.wait(T, "sS", psT_evac)
                for c in range(8):
                    fn = lambda e, c=c, o=lnt: e.transpose(
                        psT[:, c * 128:(c + 1) * 128], o[:, c * 128:(c + 1) * 128], identb[:])
                    if c == 7:
                        tr_done = k.op(T, fn, inc="tS")
                    else:
                        k.op(T, fn)
                k.wait(SE, "tS", tr_done)
                if st_ >= 2 and ti == 0:
                    k.wait(SE, "tS", strip_mm[st_ - 2])
                psT_evac = k.op(SE, lambda e, ti=ti, off=lnb_off: e.copy(
                    out=ARN[:, off:off + 4096].rearrange("p (c w) -> p c w", w=512)
                        [:, :, ti * 128:(ti + 1) * 128],
                    in_=psT[:].rearrange("p (c w) -> p c w", w=128)), inc="sS")
            ln1_done[st_] = psT_evac

            for grp in range(6):
                wpar = grp % 2
                woff = WS0 + wpar * 4096
                k.wait(G, "tS", wbuf_free[wpar])
                lds = None
                for c in range(8):
                    lds = dG(lambda e, c=c, g=grp, o=woff: e.dma_start(
                        out=ARN[:, o + c * 512:o + (c + 1) * 512],
                        in_=wqkv_h[c * 128:(c + 1) * 128, g * 512:(g + 1) * 512]))
                k.wait(T, "dG", lds)
                k.wait(T, "sS", ln1_done[st_])
                if grp < 4:
                    for jj in range(4):
                        j = grp * 4 + jj
                        pi = rot % 6
                        rot += 1
                        ps = QKV_PS[pi]
                        k.wait(T, "vS", ps_free[pi])
                        for c in range(8):
                            fn = lambda e, c=c, jj=jj, ps=ps, o=woff, lb=lnb_off: e.matmul(
                                ps, ARN[:, o + c * 512 + jj * 128:o + c * 512 + jj * 128 + 128],
                                ARN[:, lb + c * 512:lb + c * 512 + 512],
                                start=(c == 0), stop=(c == 7))
                            if c == 7:
                                mm_done = k.op(T, fn, inc="tS")
                            else:
                                k.op(T, fn)
                        base = Q0 if j < 8 else K0
                        cc = j if j < 8 else j - 8
                        doff = base + cc * 2048 + st_ * 512
                        k.wait(V, "tS", mm_done)
                        ev = k.op(V, lambda e, ps=ps, do=doff, j=j: e.tensor_scalar(
                            ARN[:, do:do + 512], ps, bqkv[:, j:j + 1], None,
                            OP.add, OP.bypass), inc="vS")
                        ps_free[pi] = ev
                else:
                    vg = grp - 4
                    for ti in range(4):
                        pi = rot % 6
                        rot += 1
                        ps = QKV_PS[pi]
                        k.wait(T, "vS", ps_free[pi])
                        for c in range(8):
                            k.op(T, lambda e, c=c, ti=ti, ps=ps, o=woff, lb=lnb_off: e.matmul(
                                ps, ARN[:, lb + c * 512 + ti * 128:
                                        lb + c * 512 + ti * 128 + 128],
                                ARN[:, o + c * 512:o + (c + 1) * 512],
                                start=(c == 0), stop=False))
                        k.wait(T, "dG", consts_dG)
                        mm_done = k.op(T, lambda e, vg=vg, ps=ps: e.matmul(
                            ps, onesb[0:1, 0:128], rowbank[0:1, vg * 512:(vg + 1) * 512],
                            start=False, stop=True), inc="tS")
                        jt = st_ * 4 + ti
                        voff = V0 + jt * 1040 + vg * 8 * 65
                        k.wait(V, "tS", mm_done)
                        ev = k.op(V, lambda e, ps=ps, vo=voff: e.tensor_copy(
                            ARN[:, vo:vo + 520].rearrange("p (h r) -> p h r", r=65)
                                [:, :, 0:64],
                            ps.rearrange("p (h dh) -> p h dh", dh=64)), inc="vS")
                        ps_free[pi] = ev
                wbuf_free[wpar] = mm_done
                strip_mm[st_] = mm_done
        k.op(V, lambda e: e.memset(ARN[:, V0:V0 + 16 * 1040].rearrange(
            "p (j h r) -> p j h r", h=16, r=65)[:, :, :, 64:65], 1.0), inc="vS")

        ra_off, rb_off = XS0, XS0 + 512
        psX_free = 0
        for base in ((Q0, K0) if _ROPE else ()):
            for c in range(8):
                for st_ in range(1, 4):
                    lo = max(PFX, st_ * 512)
                    hi = (st_ + 1) * 512
                    w = hi - lo
                    soff = base + c * 2048 + lo
                    coff = SC0 + NR + (lo - PFX)
                    sioff = SC0 + (lo - PFX)
                    va = k.op(V, lambda e, so=soff, co=coff, w=w: e.tensor_tensor(
                        ARN[:, ra_off:ra_off + w], ARN[:, so:so + w],
                        ARN[:, co:co + w], op=OP.mult), inc="vS")
                    k.wait(T, "vS", max(va, psX_free))
                    mm = k.op(T, lambda e, so=soff, w=w: e.matmul(
                        psX[:, 0:w], swapb[:], ARN[:, so:so + w], start=True, stop=True),
                        inc="tS")
                    k.wait(V, "tS", mm)
                    vpx = k.op(V, lambda e, si=sioff, w=w: e.tensor_tensor(
                        ARN[:, rb_off:rb_off + w], psX[:, 0:w],
                        ARN[:, si:si + w], op=OP.mult), inc="vS")
                    psX_free = vpx
                    k.op(V, lambda e, so=soff, w=w: e.tensor_tensor(
                        ARN[:, so:so + w], ARN[:, ra_off:ra_off + w],
                        ARN[:, rb_off:rb_off + w], op=OP.add), inc="vS")
        rope_done = k.cnt["vS"]

    if _PHASES >= 3:
        sq_off = XS0
        k.wait(SE, "vS", rope_done)
        for c in range(8):
            for which, base in ((0, Q0), (1, K0)):
                sv = k.op(SE, lambda e, b=base, c=c: e.activation(
                    out=ARN[:, sq_off:sq_off + 2048], in_=ARN[:, b + c * 2048:b + c * 2048 + 2048],
                    func=AF.Square), inc="sS")
                k.wait(T, "sS", sv)
                for st_ in range(4):
                    k.wait(T, "vS", psX_free)
                    mm = k.op(T, lambda e, st_=st_: e.matmul(
                        psX[0:1, 0:512], onesb[:, 0:1],
                        ARN[:, sq_off + st_ * 512:sq_off + (st_ + 1) * 512],
                        start=True, stop=True), inc="tS")
                    k.wait(V, "tS", mm)
                    if st_ == 0:
                        k.op(V, lambda e, w=which: e.tensor_scalar(
                            mrow[0:1, 0:512], psX[0:1, 0:512], 1.0, None, OP.mult,
                            OP.max, accum_out=misc[0:1, w:w + 1]), inc="vS")
                    else:
                        k.op(V, lambda e, w=which: e.tensor_scalar(
                            mrow[0:1, 0:512], psX[0:1, 0:512], 1.0, misc[0:1, w:w + 1],
                            OP.mult, OP.max, accum_out=misc[0:1, w:w + 1]), inc="vS")
                    k.wait(V, "vS")
                    psX_free = k.cnt["vS"]
            k.op(V, lambda e: e.tensor_tensor(misc[0:1, 2:3], misc[0:1, 0:1],
                                              misc[0:1, 1:2], op=OP.mult), inc="vS")
            k.wait(SE, "vS")
            ssv = k.op(SE, lambda e: e.activation(out=misc[0:1, 3:4], in_=misc[0:1, 2:3],
                                                  func=AF.Sqrt), inc="sS")
            k.wait(V, "sS", ssv)
            vneg = k.op(V, lambda e: e.tensor_scalar(miscb[0:1, 0:1], misc[0:1, 3:4],
                                                     -0.09375, None, OP.mult, OP.bypass),
                        inc="vS")
            vlnb = k.op(V, lambda e: e.tensor_scalar(misc[0:1, 5:6], misc[0:1, 3:4],
                                                     0.046875, None, OP.mult, OP.bypass),
                        inc="vS")
            k.wait(SE, "vS", vlnb)
            sbe = k.op(SE, lambda e: e.activation(out=misc[0:1, 6:7], in_=misc[0:1, 5:6],
                                                  func=AF.Exp), inc="sS")
            k.wait(T, "vS", max(vneg, psX_free))
            k.wait(T, "sS", sbe)
            k.wait(T, "gS", onesf_gS)
            mm = k.op(T, lambda e: e.matmul(psX[:, 0:1], onesb[0:1, 0:128],
                                            miscb[0:1, 0:1], start=True, stop=True), inc="tS")
            k.op(T, lambda e: e.matmul(psX[0:64, 1:2], onesf[0:1, 0:64],
                                       misc[0:1, 6:7], start=True, stop=True))
            mm2 = k.op(T, lambda e: e.matmul(psX[0:64, 2:3], onesf[0:1, 0:64],
                                             misc[0:1, 5:6], start=True, stop=True), inc="tS")
            k.wait(SE, "tS", mm2)
            sv2 = k.op(SE, lambda e, c=c: e.copy(out=misc[:, 16 + c:17 + c], in_=psX[:, 0:1]),
                       inc="sS")
            sv3 = k.op(SE, lambda e, c=c: e.copy(out=betatA[:, 2 * c:2 * c + 2],
                                                 in_=psX[0:64, 1:3]), inc="sS")
            psX_free_s = sv3
            k.wait(T, "sS", sv3)
        mprep_sS = k.cnt["sS"]
        mprep_vS = k.cnt["vS"]

    if _PHASES >= 4:
        PT0 = WS0
        k.wait(T, "vS", k.cnt["vS"])
        k.wait(T, "sS", k.cnt["sS"])
        state = {"psSS_free": 0, "pt_free": [0, 0], "psO_free": 0,
                 "psX_free2": 0, "otmp_free": 0, "pend": None, "pend_pv": None}
        wstate["fence"] = k.cnt["tS"]

        def emit_tail(c, st_, pv_cnt):
            hA, hB = 2 * c, 2 * c + 1
            k.wait(V, "tS", pv_cnt)
            rev = k.op(V, lambda e: e.tensor_scalar(
                r_sb[64:65, 0:1024], psO[64:65, 0:1024], 1.0, 1e-37,
                OP.mult, OP.max), inc="vS")
            um = 0
            for half in range(2):
                k.wait(T, "vS", rev)
                k.wait(T, "sS", state["psX_free2"])
                mmb = k.op(T, lambda e, half=half: e.matmul(
                    psX[0:64, 0:512], onesb[64:65, 0:64],
                    r_sb[64:65, half * 512:(half + 1) * 512], start=True, stop=True),
                    inc="tS")
                k.wait(SE, "tS", mmb)
                state["psX_free2"] = k.op(SE, lambda e, c2=2 * c: e.activation(
                    out=recipb[:, 0:512], in_=psX[0:64, 0:512], func=AF.Ln,
                    scale=betatA[:, c2:c2 + 1]), inc="sS")
                sxp = k.op(SE, lambda e, c2=2 * c: e.activation(
                    out=recip[:, 0:512], in_=recipb[:, 0:512], func=AF.Exp,
                    scale=-1.0, bias=betatA[:, c2 + 1:c2 + 2]), inc="sS")
                k.wait(V, "sS", sxp)
                if half == 0:
                    doff = Q0 + c * 2048 + st_ * 512
                    dst = ARN[0:64, doff:doff + 512]
                else:
                    k.wait(V, "dY", state["otmp_free"])
                    dst = otmp[0:64, 0:512]
                um = k.op(V, lambda e, d=dst, half=half: e.tensor_tensor(
                    d, psO[0:64, half * 512:(half + 1) * 512], recip[:, 0:512],
                    op=OP.mult), inc="vS")
            state["psO_free"] = um
            k.wait(Y, "vS", um)
            state["otmp_free"] = dY(lambda e, c=c, st_=st_: e.dma_start(
                out=ARN[64:128, Q0 + c * 2048 + st_ * 512:
                        Q0 + c * 2048 + st_ * 512 + 512],
                in_=otmp[0:64, 0:512]))

        for c in range(8):
            for st_ in range(4):
                lastpv = [None]

                def emit_pvpair(jp, ex, pto, c=c):
                    k.wait(T, "sS", ex)
                    if jp == 0:
                        k.wait(T, "vS", state["psO_free"])
                    pv = None
                    for jj in range(2):
                        j = 2 * jp + jj
                        for half in range(2):
                            h = 2 * c + half
                            fn = lambda e, h=h, j=j, jj=jj, half=half, po=pto: e.matmul(
                                psO[0:65, half * 512:(half + 1) * 512],
                                ARN[:, V0 + j * 1040 + h * 65:V0 + j * 1040 + h * 65 + 65],
                                ARN[:, po + jj * 1024 + half * 512:
                                    po + jj * 1024 + half * 512 + 512],
                                start=(j == 0), stop=(j == 15))
                            if jj == 1 and half == 1:
                                pv = k.op(T, fn, inc="tS")
                            else:
                                k.op(T, fn)
                    state["pt_free"][jp % 2] = pv
                    return pv

                for jp in range(8):
                    par = jp % 2
                    k.wait(T, "sS", state["psSS_free"])
                    mmS = None
                    for jj in range(2):
                        j = 2 * jp + jj
                        for half in range(2):
                            rw = half * 64
                            fn = lambda e, c=c, rw=rw, j=j, jj=jj, st_=st_, half=half: e.matmul(
                                psSS[:, jj * 1024 + half * 512:jj * 1024 + half * 512 + 512],
                                ARN[rw:rw + 64, K0 + c * 2048 + j * 128:
                                    K0 + c * 2048 + j * 128 + 128],
                                ARN[rw:rw + 64, Q0 + c * 2048 + st_ * 512:
                                    Q0 + c * 2048 + st_ * 512 + 512],
                                start=True, stop=True)
                            if jj == 1 and half == 1:
                                mmS = k.op(T, fn, inc="tS")
                            else:
                                k.op(T, fn)
                    if jp == 0 and state["pend_pv"] is not None:
                        state["pend_pv"]()
                        state["pend_pv"] = None
                    if jp == 1:
                        if state["pend"] is not None:
                            state["pend"]()
                            state["pend"] = None
                        lastpv[0]()
                    elif jp >= 2:
                        lastpv[0]()
                    pto = PT0 + par * 2048
                    k.wait(SE, "tS", max(mmS, state["pt_free"][par]))
                    ex = k.op(SE, lambda e, po=pto, bc=16 + c: e.activation(
                        out=ARN[:, po:po + 2048], in_=psSS[:], func=AF.Exp,
                        bias=misc[:, bc:bc + 1], scale=0.125), inc="sS")
                    state["psSS_free"] = ex
                    lastpv[0] = (lambda jp=jp, ex=ex, pto=pto, f=emit_pvpair: f(jp, ex, pto))
                    if jp == 3 or jp == 5:
                        emit_wjob()
                    if jp == 7 and (c * 4 + st_) % 8 == 7:
                        emit_wjob()
                state["pend_pv"] = lastpv[0]

                def mk_tail(c=c, st_=st_):
                    def f():
                        emit_tail(c, st_, k.cnt["tS"])
                    return f
                state["pend"] = mk_tail()
        if state["pend_pv"] is not None:
            state["pend_pv"]()
            state["pend_pv"] = None
        if state["pend"] is not None:
            state["pend"]()
            state["pend"] = None
        while wstate["i"] < len(wjobs):
            emit_wjob()
        w_store_dY = max(wstate["st"])
        s_done_dY = k.cnt["dY"]
        s_done_vS = k.cnt["vS"]
        s_done_sS = k.cnt["sS"]

    if _PHASES >= 5:
        k.wait(G, "sS", s_done_sS)
        k.wait(G, "tS", k.cnt["tS"])
        ldp = None
        for c in range(8):
            ldp = dG(lambda e, c=c: e.dma_start(
                out=ARN[:, WS0 + c * 1024:WS0 + (c + 1) * 1024],
                in_=wproj_h[c * 128:(c + 1) * 128, :]))
        k.wait(T, "dG", ldp)
        k.wait(T, "dY", s_done_dY)
        k.wait(T, "vS", s_done_vS)
        k.wait(T, "sS", s_done_sS)
        PPS = (psSS[:, 0:512], psSS[:, 512:1024], psSS[:, 1024:1536], psSS[:, 1536:2048])
        prot = 0
        pfree = [0] * 4
        x2ev = 0
        lastvln2 = [0, 0, 0]
        for qt in range(16):
            par = qt % 3
            xv = xt[:, par * D:par * D + D]
            if qt >= 3:
                k.wait(Y, "vS", lastvln2[par])
            elif qt < 2:
                k.wait(Y, "vS", lastvln[par])
            dx = dY(lambda e, x=xv, qt=qt: e.dma_start(out=x, in_=x_h[qt * 128:(qt + 1) * 128, :]))
            for s2 in range(2):
                pi = prot % 4
                prot += 1
                ps = PPS[pi]
                k.wait(T, "vS", pfree[pi])
                for c in range(8):
                    k.op(T, lambda e, c=c, qt=qt, s2=s2, ps=ps: e.matmul(
                        ps, ARN[:, Q0 + c * 2048 + qt * 128:Q0 + c * 2048 + qt * 128 + 128],
                        ARN[:, WS0 + c * 1024 + s2 * 512:WS0 + c * 1024 + s2 * 512 + 512],
                        start=(c == 0), stop=False))
                mmp = k.op(T, lambda e, s2=s2, ps=ps: e.matmul(
                    ps, onesb[0:1, 0:128], rowbank[0:1, 1024 + s2 * 512:1536 + s2 * 512],
                    start=False, stop=True), inc="tS")
                k.wait(V, "tS", mmp)
                k.wait(V, "dY", dx)
                x2ev = k.op(V, lambda e, ps=ps, x=xv, s2=s2: e.tensor_tensor(
                    x[:, s2 * 512:(s2 + 1) * 512], ps, x[:, s2 * 512:(s2 + 1) * 512],
                    op=OP.add), inc="vS")
                pfree[pi] = x2ev
            k.wait(Y, "vS", x2ev)
            dY(lambda e, x=xv, qt=qt: e.dma_start(out=x2_d[qt * 128:(qt + 1) * 128, :], in_=x))
            k.op(V, lambda e, x=xv: e.bn_stats(stats[:, 0:6], x[:, 0:512]), inc="vS")
            k.wait(V, "vS")
            k.op(V, lambda e, x=xv: e.bn_stats(stats[:, 6:12], x[:, 512:1024]), inc="vS")
            k.wait(V, "vS")
            k.op(V, lambda e: e.bn_aggr(mv[:], stats[:]), inc="vS")
            k.wait(V, "vS")
            veps = k.op(V, lambda e: e.tensor_scalar(mv[:, 1:2], mv[:, 1:2], 1e-5, None,
                                                     OP.add, OP.bypass), inc="vS")
            k.wait(SE, "vS", veps)
            ssq = k.op(SE, lambda e: e.activation(out=rstd[:, 0:1], in_=mv[:, 1:2],
                                                  func=AF.Sqrt), inc="sS")
            k.wait(V, "sS", ssq)
            k.op(V, lambda e: e.reciprocal(rstd[:, 1:2], rstd[:, 0:1]), inc="vS")
            k.wait(V, "vS")
            ln2 = ARNF[:, K0 // 2:K0 // 2 + 1024]
            ln2q = ARN[:, K0 + 2048:K0 + 3072]
            k.wait(V, "sS", qtrack["s"])
            vln2 = k.op(V, lambda e, x=xv, o=ln2: e.tensor_scalar(
                o, x, mv[:, 0:1], rstd[:, 1:2], OP.subtract, OP.mult), inc="vS")
            lastvln2[par] = vln2
            vq, _, _ = quant(ln2, ln2q, 1024, src_v=vln2)
            k.wait(Y, "vS", vq)
            dY(lambda e, q=ln2q, qt=qt: e.dma_start(
                out=ln2q_d[qt * 128:(qt + 1) * 128, :], in_=q))
        p_done_dY = k.cnt["dY"]
        f0_done_dY = k.cnt["dY"]

    if _PHASES >= 7:
        k.wait(Y, "dY", f0_done_dY)
        k.wait(Y, "vS", k.cnt["vS"])
        k.wait(Y, "sS", k.cnt["sS"])
        ldw = None
        for c in range(8):
            ldw = dY(lambda e, c=c: e.dma_start(
                out=ARN[:, FG0 + c * 2816:FG0 + (c + 1) * 2816],
                in_=wgq_d[:, c * 128:(c + 1) * 128], transpose=True))
        for c in range(8):
            ldw = dY(lambda e, c=c: e.dma_start(
                out=ARN[:, FU0 + c * 2816:FU0 + (c + 1) * 2816],
                in_=wuq_d[:, c * 128:(c + 1) * 128], transpose=True))
        ISTR = [(0, 512), (512, 1024), (1024, 1536), (1536, 2048), (2048, 2560),
                (2560, 2816)]
        k.wait(T, "vS", k.cnt["vS"])
        k.wait(T, "sS", k.cnt["sS"])
        FPS = (psSS[:, 0:512], psSS[:, 512:1024], psSS[:, 1024:1536], psSS[:, 1536:2048])
        frot = 0
        ffree = [(None, 0)] * 4
        lnT_used = {}
        hq_store = {}
        hbuf = ARN[:, FH0:FH0 + 2816]
        sgb = ARN[:, FH0 + 2816:FH0 + 5632]
        hqb = ARN[:, FH0 + 5632:FH0 + 8448]
        for g in range(4):
            lpar = g % 2
            lnT_off = FL0 + lpar * 4096
            if g >= 2:
                k.wait(Y, "tS", lnT_used[g - 2])
            ldt = None
            for c in range(8):
                ldt = dY(lambda e, c=c, g=g, o=lnT_off: e.dma_start(
                    out=ARN[:, o + c * 512:o + (c + 1) * 512],
                    in_=ln2q_d[g * 512:(g + 1) * 512, c * 128:(c + 1) * 128], transpose=True))
            k.wait(T, "dY", ldt)
            if g == 0:
                k.wait(T, "dY", ldw)
            for ti in range(4):
                tt_ = g * 4 + ti
                if tt_ >= 1:
                    k.wait(V, "dY", hq_store[tt_ - 1])
                mmu = None
                for (i0, i1) in ISTR:
                    w = i1 - i0
                    pi = frot % 4
                    frot += 1
                    ps = FPS[pi]
                    snm, scnt = ffree[pi]
                    if scnt:
                        k.wait(T, snm, scnt)
                    for c in range(8):
                        fn = lambda e, c=c, ti=ti, i0=i0, w=w, ps=ps, o=lnT_off: e.matmul(
                            ps[:, 0:w], ARN[:, o + c * 512 + ti * 128:o + c * 512 + ti * 128 + 128],
                            ARN[:, FG0 + c * 2816 + i0:FG0 + c * 2816 + i0 + w],
                            start=(c == 0), stop=(c == 7))
                        if c == 7:
                            mmg = k.op(T, fn, inc="tS")
                        else:
                            k.op(T, fn)
                    k.wait(SE, "tS", mmg)
                    k.wait(SE, "vS", qtrack["v"])
                    sgd = k.op(SE, lambda e, ps=ps, i0=i0, w=w: e.activation(
                        out=sgb[:, i0:i0 + w], in_=ps[:, 0:w], func=AF.Silu), inc="sS")
                    ffree[pi] = ("sS", sgd)
                    pi = frot % 4
                    frot += 1
                    ps = FPS[pi]
                    snm, scnt = ffree[pi]
                    if scnt:
                        k.wait(T, snm, scnt)
                    for c in range(8):
                        fn = lambda e, c=c, ti=ti, i0=i0, w=w, ps=ps, o=lnT_off: e.matmul(
                            ps[:, 0:w], ARN[:, o + c * 512 + ti * 128:o + c * 512 + ti * 128 + 128],
                            ARN[:, FU0 + c * 2816 + i0:FU0 + c * 2816 + i0 + w],
                            start=(c == 0), stop=(c == 7))
                        if c == 7:
                            mmu = k.op(T, fn, inc="tS")
                        else:
                            k.op(T, fn)
                    k.wait(V, "tS", mmu)
                    k.wait(V, "sS", sgd)
                    hdone = k.op(V, lambda e, ps=ps, i0=i0, w=w: e.tensor_tensor(
                        hbuf[:, i0:i0 + w], sgb[:, i0:i0 + w], ps[:, 0:w], op=OP.mult),
                        inc="vS")
                    ffree[pi] = ("vS", hdone)
                lnT_used[g] = mmu
                hfin = k.cnt["vS"]
                quant(hbuf[:, 0:1408], hqb[:, 0:1408], 1408, src_v=hfin)
                vq, _, _ = quant(hbuf[:, 1408:2816], hqb[:, 1408:2816], 1408, src_v=hfin)
                k.wait(Y, "vS", vq)
                hq_store[tt_] = dY(lambda e, q=hqb, tt_=tt_: e.dma_start(
                    out=hq_d[tt_ * 128:(tt_ + 1) * 128, :], in_=q))
        f1_done_dY = k.cnt["dY"]

    if _PHASES >= 8:
        k.wait(Y, "dY", f1_done_dY)
        k.wait(Y, "tS", k.cnt["tS"])
        ldwd = None
        for i in range(22):
            ldwd = dY(lambda e, i=i: e.dma_start(
                out=ARN[:, FU0 + i * 1024:FU0 + (i + 1) * 1024],
                in_=wdq_d[:, i * 128:(i + 1) * 128], transpose=True))
        k.wait(T, "vS", k.cnt["vS"])
        k.wait(T, "sS", k.cnt["sS"])
        hT_used = {}
        out_dma = [0, 0]
        oev_half = [0, 0]
        lastout = [0, 0]
        for g in range(4):
            hpar = g % 2
            hT_off = FG0 + hpar * 11264
            if g >= 2:
                k.wait(Y, "tS", hT_used[g - 2])
            ldh = None
            for i in range(22):
                ldh = dY(lambda e, i=i, g=g, o=hT_off: e.dma_start(
                    out=ARN[:, o + i * 512:o + (i + 1) * 512],
                    in_=hq_d[g * 512:(g + 1) * 512, i * 128:(i + 1) * 128], transpose=True))
            k.wait(T, "dY", ldh)
            if g == 0:
                k.wait(T, "dY", ldwd)
            for ti in range(4):
                tt_ = g * 4 + ti
                par = tt_ % 2
                xv = xt[:, par * D:par * D + D]
                ov = ot[:, par * D:par * D + D]
                if tt_ >= 2:
                    k.wait(Y, "dY", out_dma[par])
                    k.wait(Y, "vS", lastout[par])
                dx = dY(lambda e, x=xv, tt_=tt_: e.dma_start(
                    out=x, in_=x2_d[tt_ * 128:(tt_ + 1) * 128, :]))
                mmd = None
                oev = 0
                for s2 in range(2):
                    ps = psSS[:, 0:512] if s2 == 0 else psSS[:, 512:1024]
                    k.wait(T, "vS", oev_half[s2])
                    for i in range(22):
                        fn = lambda e, i=i, ti=ti, s2=s2, ps=ps, o=hT_off: e.matmul(
                            ps, ARN[:, o + i * 512 + ti * 128:o + i * 512 + ti * 128 + 128],
                            ARN[:, FU0 + i * 1024 + s2 * 512:FU0 + i * 1024 + s2 * 512 + 512],
                            start=(i == 0), stop=(i == 21))
                        if i == 21:
                            mmd = k.op(T, fn, inc="tS")
                        else:
                            k.op(T, fn)
                    k.wait(V, "tS", mmd)
                    k.wait(V, "dY", dx)
                    oev = k.op(V, lambda e, ps=ps, x=xv, o=ov, s2=s2: e.tensor_tensor(
                        o[:, s2 * 512:(s2 + 1) * 512], ps, x[:, s2 * 512:(s2 + 1) * 512],
                        op=OP.add), inc="vS")
                    oev_half[s2] = oev
                hT_used[g] = mmd
                lastout[par] = oev
                k.wait(Y, "vS", oev)
                out_dma[par] = dY(lambda e, o=ov, tt_=tt_: e.dma_start(
                    out=out_h[tt_ * 128:(tt_ + 1) * 128, :], in_=o))
        k.wait(Y, "dY", k.cnt["dY"])

    if _DEBUG:
        k.wait(Y, "vS", k.cnt["vS"])
        k.wait(Y, "sS", k.cnt["sS"])
        k.wait(Y, "gS", k.cnt["gS"])
        k.wait(Y, "tS", k.cnt["tS"])
        k.wait(Y, "dG", k.cnt["dG"])
        k.wait(Y, "dY", k.cnt["dY"])
        if "wq" in _DEBUG:
            for nm, ap, shp in (("d_wgq", wgq_d, [DI, D]), ("d_wuq", wuq_d, [DI, D]),
                                ("d_wdq", wdq_d, [D, DI])):
                o = nc.dram_tensor(nm, shp, BF16, kind="ExternalOutput").ap()
                dY(lambda e, o=o, a=ap: e.dma_start(out=o, in_=a))
        if "ln1" in _DEBUG:
            o = nc.dram_tensor("d_ln1", [128, 8192], BF16, kind="ExternalOutput").ap()
            dY(lambda e, o=o: e.dma_start(out=o, in_=ARN[:, LN0:LN0 + 8192]))
        if "qkv" in _DEBUG:
            for nm, off, w in (("d_q", Q0, 16384), ("d_k", K0, 16384), ("d_v", V0, 16640)):
                o = nc.dram_tensor(nm, [128, w], BF16, kind="ExternalOutput").ap()
                dY(lambda e, o=o, off=off, w=w: e.dma_start(out=o, in_=ARN[:, off:off + w]))
        if "att" in _DEBUG:
            o = nc.dram_tensor("d_att", [128, 16384], BF16, kind="ExternalOutput").ap()
            dY(lambda e, o=o: e.dma_start(out=o, in_=ARN[:, Q0:Q0 + 16384]))
        if "x2" in _DEBUG:
            o = nc.dram_tensor("d_x2", [S, D], F32, kind="ExternalOutput").ap()
            dY(lambda e, o=o: e.dma_start(out=o, in_=x2_d))
        if "ln2q" in _DEBUG:
            o = nc.dram_tensor("d_ln2q", [S, D], BF16, kind="ExternalOutput").ap()
            dY(lambda e, o=o: e.dma_start(out=o, in_=ln2q_d))
        if "hq" in _DEBUG:
            o = nc.dram_tensor("d_hq", [S, DI], BF16, kind="ExternalOutput").ap()
            dY(lambda e, o=o: e.dma_start(out=o, in_=hq_d))
        k.wait(Y, "dY", k.cnt["dY"])

    with nc.Block() as block:
        @block.tensor
        def _(eng):
            for fn in k.prog["T"]:
                fn(eng)

        @block.vector
        def _(eng):
            for fn in k.prog["V"]:
                fn(eng)

        @block.scalar
        def _(eng):
            for fn in k.prog["S"]:
                fn(eng)

        @block.gpsimd
        def _(eng):
            for fn in k.prog["G"]:
                fn(eng)

        @block.sync
        def _(eng):
            for fn in k.prog["Y"]:
                fn(eng)

    stk.close()
    return nc


def _host_inputs(full):
    x = np.ascontiguousarray(full["L0_attn_x"], dtype=np.float32)
    sinT = np.ascontiguousarray(np.asarray(full["L0_attn_sin"], dtype=np.float32).T)
    cosT = np.ascontiguousarray(np.asarray(full["L0_attn_cos"], dtype=np.float32).T)
    wqkv = np.ascontiguousarray(full["L0_attn_w_qkv"], dtype=np.float32)
    b_qkv = np.asarray(full["L0_attn_b_qkv"], dtype=np.float32)
    bqkv = np.ascontiguousarray(b_qkv.reshape(24, 128).T)
    brows = np.ascontiguousarray(
        np.concatenate([b_qkv[2048:3072],
                        np.asarray(full["L0_attn_b_proj"], dtype=np.float32)]).reshape(1, 2048))
    wproj = np.ascontiguousarray(full["L0_attn_w_proj"], dtype=np.float32)
    wg = np.ascontiguousarray(full["L0_ffn_w_gate"], dtype=np.float32)
    wu = np.ascontiguousarray(full["L0_ffn_w_up"], dtype=np.float32)
    wd = np.ascontiguousarray(full["L0_ffn_w_down"], dtype=np.float32)
    ones = np.ones((128, 128), dtype=ml_dtypes.bfloat16)
    ident = np.eye(128, dtype=ml_dtypes.bfloat16)
    swap = np.zeros((128, 128), dtype=np.float32)
    for blk in range(2):
        b = blk * 64
        for m in range(32):
            swap[b + m + 32, b + m] = -1.0
            swap[b + m, b + m + 32] = 1.0
    swap = swap.astype(ml_dtypes.bfloat16)
    smaskv = np.full((128, 1), -32768, dtype=np.int16)
    maps = []
    for c in range(NCORES):
        maps.append({
            "x": np.ascontiguousarray(x[c]),
            "sinT": sinT, "cosT": cosT, "wqkv": wqkv, "bqkv_pp": bqkv,
            "brows": brows, "wproj": wproj, "wg": wg, "wu": wu, "wd": wd,
            "ones": ones, "ident": ident, "swap": swap, "smask": smaskv,
        })
    return maps


def kernel(**inputs):
    maps = _host_inputs(inputs)
    nc = build_nc()
    res = run_bass_kernel_spmd(nc, maps, core_ids=list(range(NCORES)))
    out = np.stack([res.results[c]["out"] for c in range(NCORES)], axis=0)
    return np.ascontiguousarray(out, dtype=np.float32)
